# version 49
# baseline (speedup 1.0000x reference)
import numpy as np
import ml_dtypes
from contextlib import ExitStack

import concourse.bass as bass
import concourse.mybir as mybir
from concourse.bass_utils import run_bass_kernel_spmd

F32 = mybir.dt.float32
BF16 = mybir.dt.bfloat16
AF = mybir.ActivationFunctionType
ALU = mybir.AluOpType
AX = mybir.AxisListType

NCORE = 8
TOK = 4096
D = 1024
CT = 512
NCH = TOK // CT
TT = CT // 128
KD = D // 128
EXP = 6
FE = 12
FQ = 8
HR = 512
KH = HR // 128
LN_EPS = 1e-5
BIG = 3.0e38

PX = [('e', 0), ('e', 2), ('e', 4), ('f', 0), ('f', 1), ('f', 2), ('f', 3),
      ('e', 1), ('e', 3), ('e', 5)]
LAST_P = len(PX) - 1
NPX = len(PX)


class Env:
    def __init__(self, eng, marks, sems=None, engines=None):
        self.eng = eng
        self.marks = marks
        self.sems = sems or {}
        self.engines = engines or {}
        self.counts = {k: 0 for k in
                       ('pe', 'v', 's', 'w1', 'w1g', 'w1h', 'w2', 'w3', 'w2g', 'w3g', 'gs', 'lx', 'dx', 'do', 's1', 's2',
                        'dt0', 'dt1', 'dt2', 'dt3', 'dh0', 'dh1', 'dh2', 'dh3')}

    def mark(self, sem, key, ins=None, d=1):
        self.counts[sem] += d
        if self.eng is None:
            if key is not None:
                self.marks[key] = (sem, self.counts[sem])
        elif ins is not None:
            ins.then_inc(self.sems[sem], d)
        return self.counts[sem]

    def alias(self, new_key, old_key):
        if self.eng is None:
            self.marks[new_key] = self.marks[old_key]

    def wait(self, eng, key):
        if self.eng == eng and key is not None:
            sem, val = self.marks[key]
            if val > 0:
                self.engines[eng].wait_ge(self.sems[sem], val)

    def wait_prev(self, eng):
        if self.eng == eng and self.counts[eng] > 0:
            self.engines[eng].wait_ge(self.sems[eng], self.counts[eng])


def build_nc():
    nc = bass.Bass("TRN2", target_bir_lowering=False)

    dx_ = nc.dram_tensor("x", [TOK, D], F32, kind="ExternalInput")
    dW1 = nc.dram_tensor("aW1", [EXP, D, 1536], F32, kind="ExternalInput")
    dB1 = nc.dram_tensor("aB1", [EXP * FE, 128], F32, kind="ExternalInput")
    dW2 = nc.dram_tensor("aW2", [EXP, 1536, D], F32, kind="ExternalInput")
    dB2 = nc.dram_tensor("aB2", [EXP, D], F32, kind="ExternalInput")
    dR1 = nc.dram_tensor("r1w", [D, HR], F32, kind="ExternalInput")
    dR1b = nc.dram_tensor("r1b", [1, HR], F32, kind="ExternalInput")
    dLng = nc.dram_tensor("lng", [1, HR], F32, kind="ExternalInput")
    dLnb = nc.dram_tensor("lnb", [1, HR], F32, kind="ExternalInput")
    dR2 = nc.dram_tensor("r2w", [HR, EXP], F32, kind="ExternalInput")
    dR2b = nc.dram_tensor("r2b", [1, EXP], F32, kind="ExternalInput")
    dTmp = nc.dram_tensor("temp", [1, 1], F32, kind="ExternalInput")
    dRb = nc.dram_tensor("rbias", [1, EXP], F32, kind="ExternalInput")
    dF1 = nc.dram_tensor("fW1", [D, 4096], F32, kind="ExternalInput")
    dFb1 = nc.dram_tensor("fB1", [32, 128], F32, kind="ExternalInput")
    dF2 = nc.dram_tensor("fW2", [4096, D], F32, kind="ExternalInput")
    dFb2 = nc.dram_tensor("fB2", [1, D], F32, kind="ExternalInput")
    dId = nc.dram_tensor("ident", [128, 128], F32, kind="ExternalInput")
    dOne = nc.dram_tensor("ones1", [1, 128], F32, kind="ExternalInput")
    dOneB = nc.dram_tensor("ones1b", [1, 128], BF16, kind="ExternalInput")
    dY = nc.dram_tensor("y", [TOK, D], F32, kind="ExternalOutput")
    scr1 = nc.dram_tensor("scr1", [NPX, 128, KD * 1536], BF16, kind="Internal")
    scr2 = nc.dram_tensor("scr2", [NPX, 128, FE * D], BF16, kind="Internal")

    st = ExitStack()
    e = st.enter_context

    oac = e(nc.sbuf_tensor("oac", [128, 3 * TT * D], F32))
    xnT = e(nc.sbuf_tensor("xnT", [128, 2 * KD * CT], BF16))
    xnS = e(nc.sbuf_tensor("xnS", [128, 4 * D], BF16))
    hcS = e(nc.sbuf_tensor("hcS", [128, TT * HR], F32))
    hgS = e(nc.sbuf_tensor("hgS", [128, 4 * HR], F32))
    hgG = e(nc.sbuf_tensor("hgG", [128, 4 * HR], BF16))
    hgT = e(nc.sbuf_tensor("hgT", [128, KH * CT], BF16))
    H_sb = e(nc.sbuf_tensor("H_sb", [128, FE * CT], BF16))
    w1buf = e(nc.sbuf_tensor("w1buf", [128, KD * 1536], BF16))
    w2buf = e(nc.sbuf_tensor("w2buf", [128, 2 * FE * D], BF16))
    r1buf = e(nc.sbuf_tensor("r1buf", [128, KD * HR], BF16))
    r2buf = e(nc.sbuf_tensor("r2buf", [128, KH * EXP], BF16))
    musS = e(nc.sbuf_tensor("musS", [128, TT], F32))
    varS = e(nc.sbuf_tensor("varS", [128, TT], F32))
    msqS = e(nc.sbuf_tensor("msqS", [128, TT], F32))
    rstdS = e(nc.sbuf_tensor("rstdS", [128, TT], F32))
    mu2S = e(nc.sbuf_tensor("mu2S", [128, TT], F32))
    msq2S = e(nc.sbuf_tensor("msq2S", [128, TT], F32))
    var2S = e(nc.sbuf_tensor("var2S", [128, TT], F32))
    rstd2S = e(nc.sbuf_tensor("rstd2S", [128, TT], F32))
    lgS = e(nc.sbuf_tensor("lgS", [128, TT * EXP], F32))
    expS = e(nc.sbuf_tensor("expS", [128, TT * EXP], F32))
    prS = e(nc.sbuf_tensor("prS", [128, 2 * TT * EXP], F32))
    eqbS = e(nc.sbuf_tensor("eqbS", [128, EXP], F32))
    pmS = e(nc.sbuf_tensor("pmS", [128, EXP], F32))
    nmxS = e(nc.sbuf_tensor("nmxS", [128, TT], F32))
    seS = e(nc.sbuf_tensor("seS", [128, TT], F32))
    rseS = e(nc.sbuf_tensor("rseS", [128, TT], F32))
    t1S = e(nc.sbuf_tensor("t1S", [128, TT], F32))
    t2S = e(nc.sbuf_tensor("t2S", [128, TT], F32))
    pTbf = e(nc.sbuf_tensor("pTbf", [6, 2 * CT], BF16))
    b1T = e(nc.sbuf_tensor("b1T", [128, EXP * FE], F32))
    fb1T = e(nc.sbuf_tensor("fb1T", [128, 32], F32))
    b1st = e(nc.sbuf_tensor("b1st", [EXP * FE, 128], F32))
    fb1st = e(nc.sbuf_tensor("fb1st", [32, 128], F32))
    b2sb = e(nc.sbuf_tensor("b2sb", [EXP, D], BF16))
    fb2row = e(nc.sbuf_tensor("fb2row", [1, D], BF16))
    r1brow = e(nc.sbuf_tensor("r1brow", [1, HR], BF16))
    r2brow = e(nc.sbuf_tensor("r2brow", [1, EXP], BF16))
    lngrep = e(nc.sbuf_tensor("lngrep", [128, HR], F32))
    lnbrep = e(nc.sbuf_tensor("lnbrep", [128, HR], F32))
    rbrep = e(nc.sbuf_tensor("rbrep", [128, EXP], F32))
    invt = e(nc.sbuf_tensor("invt", [128, 1], F32))
    tmpr = e(nc.sbuf_tensor("tmpr", [128, 1], F32))
    lngrow = e(nc.sbuf_tensor("lngrow", [1, HR], F32))
    lnbrow = e(nc.sbuf_tensor("lnbrow", [1, HR], F32))
    rbrow = e(nc.sbuf_tensor("rbrow", [1, EXP], F32))
    tmprow = e(nc.sbuf_tensor("tmprow", [1, 1], F32))
    idS = e(nc.sbuf_tensor("idS", [128, 128], F32))
    on1S = e(nc.sbuf_tensor("on1S", [1, 128], F32))
    on1bS = e(nc.sbuf_tensor("on1bS", [1, 128], BF16))

    psA = e(nc.psum_tensor("psA", [128, 512], F32))
    psB = e(nc.psum_tensor("psB", [128, 512], F32))
    psCD = e(nc.psum_tensor("psCD", [128, 1024], F32))
    psEF = e(nc.psum_tensor("psEF", [128, 1024], F32))
    psG = e(nc.psum_tensor("psG", [128, 512], F32))
    psH = e(nc.psum_tensor("psH", [128, 512], F32))
    DNP = [psCD, psEF]

    sems = {k: e(nc.semaphore("sem_" + k)) for k in
            ('pe', 'v', 's', 'w1', 'w1g', 'w1h', 'w2', 'w3', 'w2g', 'w3g', 'gs', 'lx', 'dx', 'do', 's1', 's2',
                        'dt0', 'dt1', 'dt2', 'dt3', 'dh0', 'dh1', 'dh2', 'dh3')}

    def w1_src(kind, idx):
        if kind == 'e':
            return dW1[idx, :, :].rearrange("(kt p) f -> p kt f", p=128)
        return dF1.rearrange("(kt p) f -> p kt f", p=128)[:, :, idx * 1024:(idx + 1) * 1024]

    def w2_src(kind, idx):
        if kind == 'e':
            return dW2[idx, :, :].rearrange("(fk p) d -> p fk d", p=128)
        return dF2.rearrange("(fk p) d -> p fk d", p=128)[:, idx * FQ:(idx + 1) * FQ, :]

    def emit(env):
        def E(en):
            return env.eng == en
        T = env.engines.get('pe')
        V = env.engines.get('v')
        S = env.engines.get('s')
        SY = env.engines.get('sync')
        G = env.engines.get('gp')

        def vop(fn, key=None, waits=()):
            for w in waits:
                env.wait('v', w)
            env.wait_prev('v')
            ins = fn(V) if E('v') else None
            return env.mark('v', key, ins)

        def sop(fn, key=None, waits=()):
            for w in waits:
                env.wait('s', w)
            env.wait_prev('s')
            ins = fn(S) if E('s') else None
            return env.mark('s', key, ins)

        def pop(fn, key=None, waits=()):
            for w in waits:
                env.wait('pe', w)
            ins = fn(T) if E('pe') else None
            return env.mark('pe', key, ins)

        bank_free = {id(b): None for b in (psA, psB, psCD, psEF, psG, psH)}

        def bwait(bank):
            env.wait('pe', bank_free[id(bank)])

        def bset(bank, key):
            bank_free[id(bank)] = key

        for dst, src in [
            (idS[:, :], dId[:, :]), (b1st[:, :], dB1[:, :]), (fb1st[:, :], dFb1[:, :]),
            (on1bS[:, :], dOneB[:, :]),
        ]:
            ins = S.dma_start(dst, src) if E('s') else None
            env.mark('lx', None, ins, 16)
        if env.eng is None:
            env.marks[('lx_abc',)] = ('lx', env.counts['lx'])
        env.wait('s', ('lx_abc',))
        for dst, src in [
            (on1S[:, :], dOne[:, :]),
            (lngrow[:, :], dLng[:, :]), (lnbrow[:, :], dLnb[:, :]),
            (rbrow[:, :], dRb[:, :]), (tmprow[:, :], dTmp[:, :]),
        ]:
            ins = S.dma_start(dst, src) if E('s') else None
            env.mark('lx', None, ins, 16)
        if env.eng is None:
            env.marks[('lx_all',)] = ('lx', env.counts['lx'])
        for dst, src in [
            (r1buf[:, :].rearrange("p (kt h) -> p kt h", kt=KD),
             dR1.rearrange("(kt p) h -> p kt h", p=128)),
            (r2buf[:, :].rearrange("p (kt s) -> p kt s", kt=KH),
             dR2.rearrange("(kt p) s -> p kt s", p=128)),
            (b2sb[:, :], dB2[:, :]),
            (fb2row[:, :], dFb2[:, :]),
            (r1brow[:, :], dR1b[:, :]),
            (r2brow[:, :], dR2b[:, :]),
        ]:
            ins = G.dma_start(dst, src) if E('gp') else None
            env.mark('gs', None, ins, 16)
        if env.eng is None:
            env.marks[('gs_all',)] = ('gs', env.counts['gs'])
            env.marks[('st1', 0)] = ('s1', 0)
            env.marks[('st2', 0)] = ('s2', 0)
        env.wait('gp', ('gs_all',))

        def setup_compute():
            pop(lambda t: t.transpose(psA[:, 0:EXP * FE], b1st[:, :], idS[0:EXP * FE, 0:EXP * FE]),
                ('pe_b1T',), waits=[('lx_abc',)])
            pop(lambda t: t.transpose(psB[:, 0:32], fb1st[:, :], idS[0:32, 0:32]),
                ('pe_fb1T',))
            pop(lambda t: t.matmul(psCD[:, 0:HR], on1S[:, :], lngrow[:, :], start=True, stop=True),
                ('pe_lng',), waits=[('lx_all',)])
            pop(lambda t: t.matmul(psCD[:, 512:512 + HR], on1S[:, :], lnbrow[:, :], start=True, stop=True),
                ('pe_lnb',))
            pop(lambda t: t.matmul(psEF[:, 0:EXP], on1S[:, :], rbrow[:, :], start=True, stop=True),
                ('pe_rb',))
            pop(lambda t: t.matmul(psEF[:, 512:513], on1S[:, :], tmprow[:, :], start=True, stop=True),
                ('pe_tmp',))
            vop(lambda v: v.tensor_copy(b1T[:, :], psA[:, 0:EXP * FE]), ('v_b1T',), waits=[('pe_b1T',)])
            vop(lambda v: v.tensor_copy(fb1T[:, :], psB[:, 0:32]), ('v_fb1T',), waits=[('pe_fb1T',)])
            vop(lambda v: v.tensor_copy(lngrep[:, :], psCD[:, 0:HR]), ('v_lng',), waits=[('pe_lng',)])
            vop(lambda v: v.tensor_copy(lnbrep[:, :], psCD[:, 512:512 + HR]), ('v_lnb',), waits=[('pe_lnb',)])
            vop(lambda v: v.tensor_copy(rbrep[:, :], psEF[:, 0:EXP]), ('v_rb',), waits=[('pe_rb',)])
            vop(lambda v: v.tensor_copy(tmpr[:, :], psEF[:, 512:513]), ('v_tmp',), waits=[('pe_tmp',)])
            vop(lambda v: v.reciprocal(invt[:, :], tmpr[:, :]), ('v_invt',))
            bset(psA, ('v_b1T',)); bset(psB, ('v_fb1T',))
            bset(psCD, ('v_lnb',)); bset(psEF, ('v_tmp',))

        xn_last = [None, None, None, None]
        hg_last = [None, None, None, None]
        UPB = [psA, psB, psH]
        upbank = {id(psA): None, id(psB): None, id(psH): None}
        upcnt = [0]

        def oview(c, i, lo=0, hi=D):
            par = c % 3
            return oac[:, (par * TT + i) * D + lo:(par * TT + i) * D + hi]

        def produce_stages(cc):
            par = cc % 2
            xo = par * KD * CT
            po = par * TT * EXP
            to = par * CT

            def xn_dmaT():
                if cc >= 2:
                    env.wait('sync', ('pe_up_all', cc - 2, LAST_P))
                for i in range(TT):
                    env.wait('sync', ('v_xn', cc, i))
                    for j in range(KD):
                        ins = (SY.dma_start_transpose(
                            xnT[:, xo + j * CT + i * 128:xo + j * CT + (i + 1) * 128],
                            xnS[:, i * D + j * 128:i * D + (j + 1) * 128])
                            if E('sync') else None)
                        key = ('dt_i', cc, i) if j == KD - 1 else None
                        env.mark('dt%d' % i, key, ins, 16)

            def hmm_pe(i):
                bk = psG
                bwait(bk)
                w = [('dt_i', cc, i)]
                if cc == 0 and i == 0:
                    w += [('gs_all',), ('lx_abc',)]
                for kt in range(KD):
                    pop(lambda t, k=kt, b=bk: t.matmul(
                        b[:, 0:HR], xnT[:, xo + k * CT + i * 128:xo + k * CT + (i + 1) * 128],
                        r1buf[:, k * HR:(k + 1) * HR], start=(k == 0), stop=False),
                        None, waits=(w if kt == 0 else []))
                pop(lambda t, b=bk: t.matmul(b[:, 0:HR], on1bS[:, :], r1brow[:, :],
                                             start=False, stop=True), ('pe_h', cc, i))

            def hmm_v(i):
                bk = psG
                w = [('pe_h', cc, i)]
                if i == 0 and cc >= 1:
                    w.append(('v_aff', cc - 1, TT - 1))
                sop(lambda s, b=bk: s.activation(hcS[:, i * HR:(i + 1) * HR], b[:, 0:HR],
                                                 AF.Copy, accum_out=mu2S[:, i:i + 1]),
                    ('s_hev', cc, i), waits=w)
                bset(bk, ('s_hev', cc, i))
                sop(lambda s: s.activation(hgS[:, D:D + HR], hcS[:, i * HR:(i + 1) * HR],
                                           AF.Square, accum_out=var2S[:, i:i + 1]),
                    ('s_hsq', cc, i))

            def hg_vs(i):
                sl = i * HR
                w = [('s_rstd2', cc)] + ([hg_last[i]] if hg_last[i] else [])
                vop(lambda v: v.tensor_scalar(hgS[:, sl:sl + HR],
                                              hcS[:, i * HR:(i + 1) * HR],
                                              mu2S[:, i:i + 1], rstd2S[:, i:i + 1],
                                              op0=ALU.subtract, op1=ALU.mult),
                    None, waits=w)
                vop(lambda v: v.tensor_mul(hgS[:, sl:sl + HR], hgS[:, sl:sl + HR],
                                           lngrep[:, :]))
                vop(lambda v: v.tensor_add(hgS[:, sl:sl + HR], hgS[:, sl:sl + HR],
                                           lnbrep[:, :]), ('v_aff', cc, i))
                sop(lambda s: s.activation(hgG[:, sl:sl + HR], hgS[:, sl:sl + HR], AF.Gelu),
                    ('s_hg', cc, i), waits=[('v_aff', cc, i)])

            def hg_dmaT():
                if cc >= 1:
                    env.wait('sync', ('pe_lg', cc - 1))
                for i in range(TT):
                    env.wait('sync', ('s_hg', cc, i))
                    for j in range(KH):
                        ins = (SY.dma_start_transpose(
                            hgT[:, j * CT + i * 128:j * CT + (i + 1) * 128],
                            hgG[:, i * HR + j * 128:i * HR + (j + 1) * 128])
                            if E('sync') else None)
                        key = ('dh_i', cc, i) if j == KH - 1 else None
                        env.mark('dh%d' % i, key, ins, 16)
                    hg_last[i] = ('pe_lg', cc)

            def pre_v_piece(i):
                w = [('dx', cc)] if i == 0 else []
                if i == 0 and cc >= 1:
                    w.append(('v_xn', cc - 1, TT - 1))
                sop(lambda s, ii=i: s.activation(hgS[:, 0:D], oview(cc, ii), AF.Copy,
                                                 accum_out=musS[:, ii:ii + 1]),
                    None, waits=w)
                sop(lambda s, ii=i: s.activation(hgS[:, 0:D], oview(cc, ii), AF.Square,
                                                 accum_out=varS[:, ii:ii + 1]),
                    ('s_acc', cc) if i == TT - 1 else None)
                if i == TT - 1:
                    vop(lambda v: v.tensor_scalar(musS[:, :], musS[:, :], 1.0 / D, None,
                                                  op0=ALU.mult), None, waits=[('s_acc', cc)])
                    vop(lambda v: v.tensor_mul(msqS[:, :], musS[:, :], musS[:, :]))
                    vop(lambda v: v.tensor_scalar(varS[:, :], varS[:, :], 1.0 / D, LN_EPS,
                                                  op0=ALU.mult, op1=ALU.add))
                    vop(lambda v: v.tensor_sub(varS[:, :], varS[:, :], msqS[:, :]),
                        ('v_var', cc))
                    sop(lambda s: s.activation(rstdS[:, :], varS[:, :], AF.Ln), None,
                        waits=[('v_var', cc)])
                    sop(lambda s: s.activation(rstdS[:, :], rstdS[:, :], AF.Exp, scale=-0.5),
                        ('s_rstd', cc))

            def s0():
                for i in range(TT):
                    w = [('s_rstd', cc)]
                    if i == 0 and cc >= 1:
                        w.extend(('dt_i', cc - 1, t) for t in range(TT))
                    if i == 0 and cc >= 3:
                        w.append(('pe_up_all', cc - 1, 0))
                    vop(lambda v, ii=i: v.tensor_scalar(xnS[:, ii * D:(ii + 1) * D],
                                                        oview(cc, ii),
                                                        musS[:, ii:ii + 1], rstdS[:, ii:ii + 1],
                                                        op0=ALU.subtract, op1=ALU.mult),
                        ('v_xn', cc, i), waits=w)
                xn_dmaT()

            def rstd2_batch():
                vop(lambda v: v.tensor_scalar(mu2S[:, :], mu2S[:, :], 1.0 / HR, None,
                                              op0=ALU.mult), None, waits=[('s_hsq', cc, TT - 1)])
                vop(lambda v: v.tensor_mul(msq2S[:, :], mu2S[:, :], mu2S[:, :]))
                vop(lambda v: v.tensor_scalar(var2S[:, :], var2S[:, :], 1.0 / HR, LN_EPS,
                                              op0=ALU.mult, op1=ALU.add))
                vop(lambda v: v.tensor_sub(var2S[:, :], var2S[:, :], msq2S[:, :]),
                    ('v_var2', cc))
                sop(lambda s: s.activation(rstd2S[:, :], var2S[:, :], AF.Ln), None,
                    waits=[('v_var2', cc)])
                sop(lambda s: s.activation(rstd2S[:, :], rstd2S[:, :], AF.Exp, scale=-0.5),
                    ('s_rstd2', cc))

            def s7():
                bwait(psG)
                for i in range(TT):
                    w = [('dh_i', cc, i)]
                    for kt in range(KH):
                        pop(lambda t, k=kt, ii=i: t.matmul(
                            psG[:, ii * EXP:(ii + 1) * EXP],
                            hgT[:, k * CT + ii * 128:k * CT + (ii + 1) * 128],
                            r2buf[:, k * EXP:(k + 1) * EXP], start=(k == 0), stop=False),
                            None, waits=(w if kt == 0 else []))
                    pop(lambda t, ii=i: t.matmul(psG[:, ii * EXP:(ii + 1) * EXP], on1bS[:, :],
                                                 r2brow[:, :], start=False, stop=True),
                        ('pe_lg', cc) if i == TT - 1 else None)
                for i in range(TT):
                    g = slice(i * EXP, (i + 1) * EXP)
                    vop(lambda v, gg=g: v.tensor_scalar(lgS[:, gg], psG[:, gg], invt[:, 0:1],
                                                        None, op0=ALU.mult),
                        ('v_lge', cc) if i == TT - 1 else None,
                        waits=[('pe_lg', cc)] if i == 0 else [])
                    vop(lambda v, gg=g: v.tensor_add(lgS[:, gg], lgS[:, gg], rbrep[:, :]))
                    vop(lambda v, gg=g, ii=i: v.reduce_max(nmxS[:, ii:ii + 1], lgS[:, gg],
                                                           axis=AX.X, negate=True),
                        ('v_nmx', cc, i))
                bset(psG, ('v_lge', cc))

            def s8a():
                for i in range(TT):
                    g = slice(i * EXP, (i + 1) * EXP)
                    sop(lambda s, gg=g, ii=i: s.activation(expS[:, gg], lgS[:, gg], AF.Exp,
                                                           bias=nmxS[:, ii:ii + 1]),
                        ('s_exp', cc, i), waits=[('v_nmx', cc, TT - 1)] if i == 0 else [])

            def s8a_piece(i):
                if True:
                    g = slice(i * EXP, (i + 1) * EXP)
                    pg = slice(po + i * EXP, po + (i + 1) * EXP)
                    vop(lambda v, gg=g, ii=i: v.reduce_sum(seS[:, ii:ii + 1], expS[:, gg],
                                                           axis=AX.X),
                        None, waits=[('s_exp', cc, i)])
                    vop(lambda v, ii=i: v.reciprocal(rseS[:, ii:ii + 1], seS[:, ii:ii + 1]))
                    vop(lambda v, gg=g, pgg=pg, ii=i: v.tensor_scalar(prS[:, pgg], expS[:, gg],
                                                                      rseS[:, ii:ii + 1],
                                                                      None, op0=ALU.mult))
                    vop(lambda v, pgg=pg, ii=i: v.reduce_max(t1S[:, ii:ii + 1], prS[:, pgg],
                                                             axis=AX.X))
                    vop(lambda v, pgg=pg, ii=i: v.tensor_scalar(eqbS[:, :], prS[:, pgg],
                                                                t1S[:, ii:ii + 1], BIG,
                                                                op0=ALU.is_equal, op1=ALU.mult))
                    vop(lambda v, pgg=pg: v.tensor_sub(pmS[:, :], prS[:, pgg], eqbS[:, :]))
                    vop(lambda v, ii=i: v.reduce_max(t2S[:, ii:ii + 1], pmS[:, :], axis=AX.X))
                    vop(lambda v, pgg=pg, ii=i: v.scalar_tensor_tensor(prS[:, pgg], prS[:, pgg],
                                                                       t2S[:, ii:ii + 1],
                                                                       prS[:, pgg],
                                                                       op0=ALU.is_ge,
                                                                       op1=ALU.mult),
                        ('v_pr', cc, i))

            def s8b():
                bwait(psG)
                for i in range(TT):
                    pop(lambda t, ii=i: t.transpose(psG[0:EXP, ii * 128:(ii + 1) * 128],
                                                    prS[:, po + ii * EXP:po + (ii + 1) * EXP],
                                                    idS[:, :]),
                        ('pe_pT', cc) if i == TT - 1 else None, waits=[('v_pr', cc, i)])
                vop(lambda v: v.tensor_copy(pTbf[:, to:to + CT], psG[0:EXP, 0:CT]),
                    ('v_pTe', cc), waits=[('pe_pT', cc)])
                bset(psG, ('v_pTe', cc))

            pre = [None] * 10
            mid = [None] * 10
            vp = [None] * 10
            vp[0] = [lambda ii=i: pre_v_piece(ii) for i in range(TT)]
            pre[1] = s0
            pre[2] = lambda: hmm_pe(0)
            mid[2] = lambda: hmm_v(0)
            pre[3] = lambda: hmm_pe(1)
            mid[3] = lambda: hmm_v(1)
            pre[4] = lambda: hmm_pe(2)
            mid[4] = lambda: hmm_v(2)
            pre[5] = lambda: hmm_pe(3)
            mid[5] = lambda: hmm_v(3)
            vp[5] = [rstd2_batch]
            vp[6] = [lambda ii=i: hg_vs(ii) for i in range(TT)]
            pre[7] = hg_dmaT
            pre[8] = s7
            mid[8] = lambda: (s8a(), s8a_piece(0), s8a_piece(1))
            vp[8] = [lambda: s8a_piece(2), lambda: s8a_piece(3)]
            pre[9] = s8b
            return pre, mid, vp

        def consume(c, produce_next):
            par = c % 2
            xo = par * KD * CT
            po = par * TT * EXP
            to = par * CT

            if c >= 1:
                pp = (c - 1) % 3
                env.wait('sync', ('v_dnall', c - 1))
                ins = (SY.dma_start(
                    dY[(c - 1) * CT:c * CT, :].rearrange("(tt p) d -> p tt d", p=128),
                    oac[:, pp * TT * D:(pp + 1) * TT * D].rearrange("p (tt d) -> p tt d", tt=TT))
                    if E('sync') else None)
                env.mark('do', ('do', c - 1), ins, 16)
            if c + 2 < NCH:
                cn = c + 2
                pn = cn % 3
                env.wait('sync', ('dx', cn - 1))
                if cn >= 3:
                    env.wait('sync', ('do', cn - 3))
                ins = (SY.dma_start(
                    oac[:, pn * TT * D:(pn + 1) * TT * D].rearrange("p (tt d) -> p tt d", tt=TT),
                    dx_[cn * CT:(cn + 1) * CT, :].rearrange("(tt p) d -> p tt d", p=128))
                    if E('sync') else None)
                env.mark('dx', ('dx', cn), ins, 16)

            for pi, (kind, idx) in enumerate(PX):
                prev = (c, pi - 1) if pi > 0 else ((c - 1, LAST_P) if c > 0 else None)
                nf = FE if kind == 'e' else FQ
                fw = 1536 if kind == 'e' else 1024
                if c == 0:
                    if prev is not None:
                        env.wait('gp', ('pe_up_all',) + prev)
                    env.wait('gp', ('st1', pi))
                    dst1 = w1buf[:, 0:KD * fw].rearrange("p (kt f) -> p kt f", kt=KD)
                    ins = G.dma_start(dst1, w1_src(kind, idx)) if E('gp') else None
                    env.mark('w1g', ('gw1', c, pi), ins, 16)
                    env.wait('gp', ('gw1', c, pi))
                    ins = G.dma_start(scr1[pi, :, 0:KD * fw], w1buf[:, 0:KD * fw]) if E('gp') else None
                    env.mark('s1', ('st1', pi + 1), ins, 16)
                if c <= 1:
                    gseq = c * NPX + pi
                    if gseq >= 2:
                        p2 = gseq - 2
                        env.wait('gp', ('pe_dn_all', p2 // NPX, p2 % NPX))
                    if c == 1 and pi >= 2:
                        env.wait('gp', ('st2s', pi - 2))
                    w2o = (gseq % 2) * FE * D
                    dst2 = w2buf[:, w2o:w2o + nf * D].rearrange("p (fk d) -> p fk d", fk=nf)
                    ins = G.dma_start(dst2, w2_src(kind, idx)) if E('gp') else None
                    env.mark('w2g' if gseq % 2 == 0 else 'w3g', ('gw2', c, pi), ins, 16)

            HKT = 2
            HC = HKT * 1536

            def emit_gw1h(cw):
                for t in range(TT):
                    env.wait('s', ('dt_i', cw, t))
                ins = S.dma_start(
                    xnS[:, 0:HC].rearrange("p (kt f) -> p kt f", kt=HKT),
                    scr1[0, :, 0:HC]) if E('s') else None
                env.mark('w1h', ('gw1h', cw), ins, 16)

            def emit_gw1(cw, pw):
                prev = (cw, pw - 1) if pw > 0 else (cw - 1, LAST_P)
                env.wait('s', ('pe_up_all',) + prev)
                if cw == 1:
                    env.wait('s', ('st1', NPX))
                fwv = 1536 if PX[pw][0] == 'e' else 1024
                lo = HC if (pw == 0 and cw >= 2) else 0
                dst1 = w1buf[:, lo:KD * fwv].rearrange("p (kt f) -> p kt f", kt=KD - (HKT if lo else 0))
                ins = S.dma_start(dst1, scr1[pw, :, lo:KD * fwv]) if E('s') else None
                env.mark('w1', ('gw1', cw, pw), ins, 16)

            def emit_scr2_store(pw):
                nfw = FE if PX[pw][0] == 'e' else FQ
                w2o = ((NPX + pw) % 2) * FE * D
                env.wait('sync', ('gw2', 1, pw))
                if pw >= 1:
                    env.wait('sync', ('st2s', pw - 1))
                ins = SY.dma_start(scr2[pw, :, 0:nfw * D], w2buf[:, w2o:w2o + nfw * D]) if E('sync') else None
                env.mark('s2', ('st2s', pw), ins, 16)

            def emit_gw2(cw, pw):
                gseq = cw * NPX + pw
                p2 = gseq - 2
                env.wait('s', ('pe_dn_all', p2 // NPX, p2 % NPX))
                if cw == 2:
                    env.wait('s', ('st2s', LAST_P))
                nfw = FE if PX[pw][0] == 'e' else FQ
                w2o = (gseq % 2) * FE * D
                dst2 = w2buf[:, w2o:w2o + nfw * D].rearrange("p (fk d) -> p fk d", fk=nfw)
                ins = S.dma_start(dst2, scr2[pw, :, 0:nfw * D]) if E('s') else None
                env.mark('w2' if gseq % 2 == 0 else 'w3', ('gw2', cw, pw), ins, 16)

            for pi, (kind, idx) in enumerate(PX):
                if produce_next is not None and produce_next[0][pi] is not None:
                    produce_next[0][pi]()
                nf = FE if kind == 'e' else FQ
                fw = 1536 if kind == 'e' else 1024
                for f in range(nf):
                    bk = UPB[upcnt[0] % 3]
                    upcnt[0] += 1
                    w = []
                    if f == 0:
                        w.append(('gw1', c, pi))
                        if pi == 0:
                            w.extend(('dt_i', c, t) for t in range(TT))
                            if c >= 2:
                                w.append(('gw1h', c))
                    if upbank[id(bk)] is not None:
                        w.append(upbank[id(bk)])
                    for kt in range(KD):
                        key = ('pe_up', c, pi, f) if kt == KD - 1 else None
                        wsrc = xnS if (c >= 2 and pi == 0 and kt < 2) else w1buf
                        pop(lambda t, k=kt, b=bk, ff=f, ws=wsrc: t.matmul(
                            b[:, :], ws[:, k * fw + ff * 128:k * fw + (ff + 1) * 128],
                            xnT[:, xo + k * CT:xo + (k + 1) * CT], start=(k == 0),
                            stop=(k == KD - 1)),
                            key, waits=(w if kt == 0 else []))
                    if kind == 'e':
                        af = AF.Gelu if idx % 2 == 0 else AF.Silu
                        sop(lambda s, b=bk, ff=f: s.activation(
                            H_sb[:, ff * CT:(ff + 1) * CT], b[:, :], af,
                            bias=b1T[:, idx * FE + ff:idx * FE + ff + 1]),
                            ('s_ue', c, pi, f),
                            waits=[('pe_up', c, pi, f)] + ([('v_b1T',)] if c == 0 and pi == 0 and f == 0 else []))
                        upbank[id(bk)] = ('s_ue', c, pi, f)
                    else:
                        sop(lambda s, b=bk, ff=f: s.activation(
                            H_sb[:, ff * CT:(ff + 1) * CT], b[:, :], AF.Relu,
                            bias=fb1T[:, idx * FQ + ff:idx * FQ + ff + 1]),
                            ('s_ue', c, pi, f),
                            waits=[('pe_up', c, pi, f)] + ([('v_fb1T',)] if c == 0 and idx == 0 and f == 0 else []))
                        upbank[id(bk)] = ('s_ue', c, pi, f)
                env.alias(('pe_up_all', c, pi), ('pe_up', c, pi, nf - 1))
                if produce_next is not None and produce_next[1][pi] is not None:
                    produce_next[1][pi]()
                if pi + 1 < NPX:
                    if c >= 1:
                        emit_gw1(c, pi + 1)
                    if c >= 1 and pi == 2 and c + 1 < NCH:
                        emit_gw1h(c + 1)
                    if c == 1:
                        emit_scr2_store(pi)
                    elif c >= 2:
                        emit_gw2(c, pi + 1)
                elif c + 1 < NCH:
                    emit_gw1(c + 1, 0)
                    if c == 1:
                        emit_scr2_store(LAST_P)
                    if c >= 1:
                        emit_gw2(c + 1, 0)
                def act_mark(fk):
                    return ('s_ue', c, pi, fk)
                w2o = ((c * NPX + pi) % 2) * FE * D
                for i in range(TT):
                    pr = DNP[i % 2]
                    bC = pr[:, 0:512]
                    bD = pr[:, 512:1024]
                    bwait(pr)
                    first = True
                    if kind == 'f' and idx == 0:
                        pop(lambda t, ii=i: t.matmul(bC,
                                                     pTbf[:, to + ii * 128:to + (ii + 1) * 128],
                                                     b2sb[:, 0:512], start=True, stop=False),
                            None, waits=([('v_pTe', c)] if i == 0 else []))
                        pop(lambda t, ii=i: t.matmul(bD,
                                                     pTbf[:, to + ii * 128:to + (ii + 1) * 128],
                                                     b2sb[:, 512:1024], start=True, stop=False))
                        pop(lambda t: t.matmul(bC, on1bS[:, :], fb2row[:, 0:512],
                                               start=False, stop=False))
                        pop(lambda t: t.matmul(bD, on1bS[:, :], fb2row[:, 512:1024],
                                               start=False, stop=False))
                        first = False
                    for fk in range(nf):
                        stv = first and fk == 0
                        spv = fk == nf - 1
                        w = []
                        if i == 0:
                            w.append(act_mark(fk))
                            if fk == 0:
                                w.append(('gw2', c, pi))
                        pop(lambda t, k=fk, s0v=stv, s1v=spv, ii=i: t.matmul(
                            bC, H_sb[:, k * CT + ii * 128:k * CT + (ii + 1) * 128],
                            w2buf[:, w2o + k * D:w2o + k * D + 512], start=s0v, stop=s1v),
                            None, waits=w)
                        key = ('pe_dn', c, pi, i) if spv else None
                        pop(lambda t, k=fk, s0v=stv, s1v=spv, ii=i: t.matmul(
                            bD, H_sb[:, k * CT + ii * 128:k * CT + (ii + 1) * 128],
                            w2buf[:, w2o + k * D + 512:w2o + (k + 1) * D], start=s0v, stop=s1v), key)
                    if kind == 'e':
                        pc = prS[:, po + i * EXP + idx:po + i * EXP + idx + 1]
                        vop(lambda v, ii=i, p_=pc, b=pr: v.scalar_tensor_tensor(
                            oview(c, ii), b[:, :], p_, oview(c, ii),
                            op0=ALU.mult, op1=ALU.add),
                            ('v_dn', c, pi, i), waits=[('pe_dn', c, pi, i)])
                    else:
                        vop(lambda v, ii=i, b=pr: v.tensor_add(oview(c, ii),
                                                               oview(c, ii), b[:, :]),
                            ('v_dn', c, pi, i), waits=[('pe_dn', c, pi, i)])
                    bset(pr, ('v_dn', c, pi, i))
                    if produce_next is not None:
                        vps = produce_next[2][pi]
                        if vps is not None and i < len(vps):
                            vps[i]()
                env.alias(('pe_dn_all', c, pi), ('pe_dn', c, pi, TT - 1))
            env.alias(('v_dnall', c), ('v_dn', c, LAST_P, TT - 1))

        ins = (SY.dma_start(
            oac[:, 0:TT * D].rearrange("p (tt d) -> p tt d", tt=TT),
            dx_[0:CT, :].rearrange("(tt p) d -> p tt d", p=128))
            if E('sync') else None)
        env.mark('dx', ('dx', 0), ins, 16)
        env.wait('sync', ('dx', 0))
        ins = (SY.dma_start(
            oac[:, TT * D:2 * TT * D].rearrange("p (tt d) -> p tt d", tt=TT),
            dx_[CT:2 * CT, :].rearrange("(tt p) d -> p tt d", p=128))
            if E('sync') else None)
        env.mark('dx', ('dx', 1), ins, 16)

        pre0, mid0, vp0 = produce_stages(0)
        for p in vp0[0]:
            p()
        for j in range(10):
            if pre0[j] is not None:
                pre0[j]()
            if mid0[j] is not None:
                mid0[j]()
            if j >= 1 and vp0[j] is not None:
                for p in vp0[j]:
                    p()
            if j == 5:
                setup_compute()
        for c in range(NCH):
            produce_next = produce_stages(c + 1) if c + 1 < NCH else None
            consume(c, produce_next)

        c = NCH - 1
        pp = c % 3
        env.wait('sync', ('v_dnall', c))
        ins = (SY.dma_start(
            dY[c * CT:(c + 1) * CT, :].rearrange("(tt p) d -> p tt d", p=128),
            oac[:, pp * TT * D:(pp + 1) * TT * D].rearrange("p (tt d) -> p tt d", tt=TT))
            if E('sync') else None)
        env.mark('do', ('do', c), ins, 16)
        env.wait('sync', ('do', c))

    return nc, st, sems, emit


def build():
    nc, st, SEMS, emit = build_nc()
    marks = {}
    emit(Env(None, marks))
    with st:
        with nc.Block() as blk:
            def run(eng_name):
                def body(engine):
                    env = Env(eng_name, marks, SEMS, {eng_name: engine})
                    emit(env)
                return body
            blk.tensor(run('pe'))
            blk.vector(run('v'))
            blk.scalar(run('s'))
            blk.sync(run('sync'))
            blk.gpsimd(run('gp'))
    return nc


_NC_CACHE = None


def kernel(**inputs):
    global _NC_CACHE
    gi = lambda k: np.ascontiguousarray(np.asarray(inputs[k], dtype=np.float32))
    x = gi("L0_attn_x").reshape(NCORE * TOK, D)
    shards = np.split(x, NCORE, axis=0)

    common = dict(
        aW1=gi("L0_attn_W1"), aB1=gi("L0_attn_b1").reshape(EXP * FE, 128),
        aW2=gi("L0_attn_W2"), aB2=gi("L0_attn_b2"),
        r1w=gi("L0_attn_r1_w"), r1b=gi("L0_attn_r1_b").reshape(1, HR),
        lng=gi("L0_attn_ln_g").reshape(1, HR), lnb=gi("L0_attn_ln_b").reshape(1, HR),
        r2w=gi("L0_attn_r2_w"), r2b=gi("L0_attn_r2_b").reshape(1, EXP),
        temp=gi("L0_attn_temp").reshape(1, 1), rbias=gi("L0_attn_route_bias").reshape(1, EXP),
        fW1=gi("L0_ffn_W1"), fB1=gi("L0_ffn_b1").reshape(32, 128),
        fW2=gi("L0_ffn_W2"), fB2=gi("L0_ffn_b2").reshape(1, D),
        ident=np.eye(128, dtype=np.float32),
        ones1=np.ones((1, 128), dtype=np.float32),
        ones1b=np.ones((1, 128), dtype=ml_dtypes.bfloat16),
    )
    if _NC_CACHE is None:
        _NC_CACHE = build()
    nc = _NC_CACHE
    in_maps = [dict(common, x=shards[ci]) for ci in range(NCORE)]
    res = run_bass_kernel_spmd(nc, in_maps, core_ids=list(range(NCORE)))
    y = np.concatenate([r["y"] for r in res.results], axis=0)
    return y.reshape(NCORE, TOK, D).astype(np.float32, copy=False)


# version 50
# speedup vs baseline: 1.1790x; 1.1790x over previous
import numpy as np
import ml_dtypes
from contextlib import ExitStack

import concourse.bass as bass
import concourse.mybir as mybir
from concourse.bass_utils import run_bass_kernel_spmd

F32 = mybir.dt.float32
BF16 = mybir.dt.bfloat16
AF = mybir.ActivationFunctionType
ALU = mybir.AluOpType
AX = mybir.AxisListType

NCORE = 8
TOK = 4096
D = 1024
CT = 512
NCH = TOK // CT
TT = CT // 128
KD = D // 128
EXP = 6
FE = 12
FQ = 8
HR = 512
KH = HR // 128
LN_EPS = 1e-5
BIG = 3.0e38

PX = [('e', 0), ('e', 2), ('e', 4), ('f', 0), ('f', 1), ('f', 2), ('f', 3),
      ('e', 1), ('e', 3), ('e', 5)]
LAST_P = len(PX) - 1
NPX = len(PX)


class Env:
    def __init__(self, eng, marks, sems=None, engines=None):
        self.eng = eng
        self.marks = marks
        self.sems = sems or {}
        self.engines = engines or {}
        self.counts = {k: 0 for k in
                       ('pe', 'v', 's', 'w1', 'w1g', 'w1h', 'w2', 'w3', 'w2g', 'w3g', 'gs', 'lx', 'dx', 'do', 's1', 's2',
                        'dt0', 'dt1', 'dt2', 'dt3', 'dh0', 'dh1', 'dh2', 'dh3')}

    def mark(self, sem, key, ins=None, d=1):
        self.counts[sem] += d
        if self.eng is None:
            if key is not None:
                self.marks[key] = (sem, self.counts[sem])
        elif ins is not None:
            ins.then_inc(self.sems[sem], d)
        return self.counts[sem]

    def alias(self, new_key, old_key):
        if self.eng is None:
            self.marks[new_key] = self.marks[old_key]

    def wait(self, eng, key):
        if self.eng == eng and key is not None:
            sem, val = self.marks[key]
            if val > 0:
                self.engines[eng].wait_ge(self.sems[sem], val)

    def wait_prev(self, eng):
        if self.eng == eng and self.counts[eng] > 0:
            self.engines[eng].wait_ge(self.sems[eng], self.counts[eng])


def build_nc():
    nc = bass.Bass("TRN2", target_bir_lowering=False)

    dx_ = nc.dram_tensor("x", [TOK, D], F32, kind="ExternalInput")
    dW1 = nc.dram_tensor("aW1", [EXP, D, 1536], F32, kind="ExternalInput")
    dB1 = nc.dram_tensor("aB1", [EXP * FE, 128], F32, kind="ExternalInput")
    dW2 = nc.dram_tensor("aW2", [EXP, 1536, D], F32, kind="ExternalInput")
    dB2 = nc.dram_tensor("aB2", [EXP, D], F32, kind="ExternalInput")
    dR1 = nc.dram_tensor("r1w", [D, HR], F32, kind="ExternalInput")
    dR1b = nc.dram_tensor("r1b", [1, HR], F32, kind="ExternalInput")
    dLng = nc.dram_tensor("lng", [1, HR], F32, kind="ExternalInput")
    dLnb = nc.dram_tensor("lnb", [1, HR], F32, kind="ExternalInput")
    dR2 = nc.dram_tensor("r2w", [HR, EXP], F32, kind="ExternalInput")
    dR2b = nc.dram_tensor("r2b", [1, EXP], F32, kind="ExternalInput")
    dTmp = nc.dram_tensor("temp", [1, 1], F32, kind="ExternalInput")
    dRb = nc.dram_tensor("rbias", [1, EXP], F32, kind="ExternalInput")
    dF1 = nc.dram_tensor("fW1", [D, 4096], F32, kind="ExternalInput")
    dFb1 = nc.dram_tensor("fB1", [32, 128], F32, kind="ExternalInput")
    dF2 = nc.dram_tensor("fW2", [4096, D], F32, kind="ExternalInput")
    dFb2 = nc.dram_tensor("fB2", [1, D], F32, kind="ExternalInput")
    dId = nc.dram_tensor("ident", [128, 128], F32, kind="ExternalInput")
    dOne = nc.dram_tensor("ones1", [1, 128], F32, kind="ExternalInput")
    dOneB = nc.dram_tensor("ones1b", [1, 128], BF16, kind="ExternalInput")
    dY = nc.dram_tensor("y", [TOK, D], F32, kind="ExternalOutput")
    scr1 = nc.dram_tensor("scr1", [NPX, 128, KD * 1536], BF16, kind="Internal")
    scr2 = nc.dram_tensor("scr2", [NPX, 128, FE * D], BF16, kind="Internal")

    st = ExitStack()
    e = st.enter_context

    oac = e(nc.sbuf_tensor("oac", [128, 3 * TT * D], F32))
    xnT = e(nc.sbuf_tensor("xnT", [128, 2 * KD * CT], BF16))
    xnS = e(nc.sbuf_tensor("xnS", [128, 4 * D], BF16))
    hcS = e(nc.sbuf_tensor("hcS", [128, TT * HR], F32))
    hgS = e(nc.sbuf_tensor("hgS", [128, 4 * HR], F32))
    hgG = e(nc.sbuf_tensor("hgG", [128, 4 * HR], BF16))
    hgT = e(nc.sbuf_tensor("hgT", [128, KH * CT], BF16))
    H_sb = e(nc.sbuf_tensor("H_sb", [128, FE * CT], BF16))
    w1buf = e(nc.sbuf_tensor("w1buf", [128, KD * 1536], BF16))
    w2buf = e(nc.sbuf_tensor("w2buf", [128, 2 * FE * D], BF16))
    r1buf = e(nc.sbuf_tensor("r1buf", [128, KD * HR], BF16))
    r2buf = e(nc.sbuf_tensor("r2buf", [128, KH * EXP], BF16))
    musS = e(nc.sbuf_tensor("musS", [128, TT], F32))
    varS = e(nc.sbuf_tensor("varS", [128, TT], F32))
    msqS = e(nc.sbuf_tensor("msqS", [128, TT], F32))
    rstdS = e(nc.sbuf_tensor("rstdS", [128, TT], F32))
    mu2S = e(nc.sbuf_tensor("mu2S", [128, TT], F32))
    msq2S = e(nc.sbuf_tensor("msq2S", [128, TT], F32))
    var2S = e(nc.sbuf_tensor("var2S", [128, TT], F32))
    rstd2S = e(nc.sbuf_tensor("rstd2S", [128, TT], F32))
    lgS = e(nc.sbuf_tensor("lgS", [128, TT * EXP], F32))
    expS = e(nc.sbuf_tensor("expS", [128, TT * EXP], F32))
    prS = e(nc.sbuf_tensor("prS", [128, 2 * TT * EXP], F32))
    eqbS = e(nc.sbuf_tensor("eqbS", [128, EXP], F32))
    pmS = e(nc.sbuf_tensor("pmS", [128, EXP], F32))
    nmxS = e(nc.sbuf_tensor("nmxS", [128, TT], F32))
    seS = e(nc.sbuf_tensor("seS", [128, TT], F32))
    rseS = e(nc.sbuf_tensor("rseS", [128, TT], F32))
    t1S = e(nc.sbuf_tensor("t1S", [128, TT], F32))
    t2S = e(nc.sbuf_tensor("t2S", [128, TT], F32))
    pTbf = e(nc.sbuf_tensor("pTbf", [6, 2 * CT], BF16))
    b1T = e(nc.sbuf_tensor("b1T", [128, EXP * FE], F32))
    fb1T = e(nc.sbuf_tensor("fb1T", [128, 32], F32))
    b1st = e(nc.sbuf_tensor("b1st", [EXP * FE, 128], F32))
    fb1st = e(nc.sbuf_tensor("fb1st", [32, 128], F32))
    b2sb = e(nc.sbuf_tensor("b2sb", [EXP, D], BF16))
    fb2row = e(nc.sbuf_tensor("fb2row", [1, D], BF16))
    r1brow = e(nc.sbuf_tensor("r1brow", [1, HR], BF16))
    r2brow = e(nc.sbuf_tensor("r2brow", [1, EXP], BF16))
    lngrep = e(nc.sbuf_tensor("lngrep", [128, HR], F32))
    lnbrep = e(nc.sbuf_tensor("lnbrep", [128, HR], F32))
    rbrep = e(nc.sbuf_tensor("rbrep", [128, EXP], F32))
    invt = e(nc.sbuf_tensor("invt", [128, 1], F32))
    tmpr = e(nc.sbuf_tensor("tmpr", [128, 1], F32))
    lngrow = e(nc.sbuf_tensor("lngrow", [1, HR], F32))
    lnbrow = e(nc.sbuf_tensor("lnbrow", [1, HR], F32))
    rbrow = e(nc.sbuf_tensor("rbrow", [1, EXP], F32))
    tmprow = e(nc.sbuf_tensor("tmprow", [1, 1], F32))
    idS = e(nc.sbuf_tensor("idS", [128, 128], F32))
    on1S = e(nc.sbuf_tensor("on1S", [1, 128], F32))
    on1bS = e(nc.sbuf_tensor("on1bS", [1, 128], BF16))

    psA = e(nc.psum_tensor("psA", [128, 512], F32))
    psB = e(nc.psum_tensor("psB", [128, 512], F32))
    psCD = e(nc.psum_tensor("psCD", [128, 1024], F32))
    psEF = e(nc.psum_tensor("psEF", [128, 1024], F32))
    psG = e(nc.psum_tensor("psG", [128, 512], F32))
    psH = e(nc.psum_tensor("psH", [128, 512], F32))
    DNP = [psCD, psEF]

    sems = {k: e(nc.semaphore("sem_" + k)) for k in
            ('pe', 'v', 's', 'w1', 'w1g', 'w1h', 'w2', 'w3', 'w2g', 'w3g', 'gs', 'lx', 'dx', 'do', 's1', 's2',
                        'dt0', 'dt1', 'dt2', 'dt3', 'dh0', 'dh1', 'dh2', 'dh3')}

    def w1_src(kind, idx):
        if kind == 'e':
            return dW1[idx, :, :].rearrange("(kt p) f -> p kt f", p=128)
        return dF1.rearrange("(kt p) f -> p kt f", p=128)[:, :, idx * 1024:(idx + 1) * 1024]

    def w2_src(kind, idx):
        if kind == 'e':
            return dW2[idx, :, :].rearrange("(fk p) d -> p fk d", p=128)
        return dF2.rearrange("(fk p) d -> p fk d", p=128)[:, idx * FQ:(idx + 1) * FQ, :]

    def emit(env):
        def E(en):
            return env.eng == en
        T = env.engines.get('pe')
        V = env.engines.get('v')
        S = env.engines.get('s')
        SY = env.engines.get('sync')
        G = env.engines.get('gp')

        def vop(fn, key=None, waits=()):
            for w in waits:
                env.wait('v', w)
            env.wait_prev('v')
            ins = fn(V) if E('v') else None
            return env.mark('v', key, ins)

        def sop(fn, key=None, waits=()):
            for w in waits:
                env.wait('s', w)
            env.wait_prev('s')
            ins = fn(S) if E('s') else None
            return env.mark('s', key, ins)

        def pop(fn, key=None, waits=()):
            for w in waits:
                env.wait('pe', w)
            ins = fn(T) if E('pe') else None
            return env.mark('pe', key, ins)

        bank_free = {id(b): None for b in (psA, psB, psCD, psEF, psG, psH)}

        def bwait(bank):
            env.wait('pe', bank_free[id(bank)])

        def bset(bank, key):
            bank_free[id(bank)] = key

        for dst, src in [
            (idS[:, :], dId[:, :]), (b1st[:, :], dB1[:, :]), (fb1st[:, :], dFb1[:, :]),
            (on1bS[:, :], dOneB[:, :]),
        ]:
            ins = S.dma_start(dst, src) if E('s') else None
            env.mark('lx', None, ins, 16)
        if env.eng is None:
            env.marks[('lx_abc',)] = ('lx', env.counts['lx'])
        env.wait('s', ('lx_abc',))
        for dst, src in [
            (on1S[:, :], dOne[:, :]),
            (lngrow[:, :], dLng[:, :]), (lnbrow[:, :], dLnb[:, :]),
            (rbrow[:, :], dRb[:, :]), (tmprow[:, :], dTmp[:, :]),
        ]:
            ins = S.dma_start(dst, src) if E('s') else None
            env.mark('lx', None, ins, 16)
        if env.eng is None:
            env.marks[('lx_all',)] = ('lx', env.counts['lx'])
        for dst, src in [
            (r1buf[:, :].rearrange("p (kt h) -> p kt h", kt=KD),
             dR1.rearrange("(kt p) h -> p kt h", p=128)),
            (r2buf[:, :].rearrange("p (kt s) -> p kt s", kt=KH),
             dR2.rearrange("(kt p) s -> p kt s", p=128)),
            (b2sb[:, :], dB2[:, :]),
            (fb2row[:, :], dFb2[:, :]),
            (r1brow[:, :], dR1b[:, :]),
            (r2brow[:, :], dR2b[:, :]),
        ]:
            ins = G.dma_start(dst, src) if E('gp') else None
            env.mark('gs', None, ins, 16)
        if env.eng is None:
            env.marks[('gs_all',)] = ('gs', env.counts['gs'])
            env.marks[('st1', 0)] = ('s1', 0)
            env.marks[('st2', 0)] = ('s2', 0)
        env.wait('gp', ('gs_all',))

        def setup_compute():
            pop(lambda t: t.transpose(psA[:, 0:EXP * FE], b1st[:, :], idS[0:EXP * FE, 0:EXP * FE]),
                ('pe_b1T',), waits=[('lx_abc',)])
            pop(lambda t: t.transpose(psB[:, 0:32], fb1st[:, :], idS[0:32, 0:32]),
                ('pe_fb1T',))
            pop(lambda t: t.matmul(psCD[:, 0:HR], on1S[:, :], lngrow[:, :], start=True, stop=True),
                ('pe_lng',), waits=[('lx_all',)])
            pop(lambda t: t.matmul(psCD[:, 512:512 + HR], on1S[:, :], lnbrow[:, :], start=True, stop=True),
                ('pe_lnb',))
            pop(lambda t: t.matmul(psEF[:, 0:EXP], on1S[:, :], rbrow[:, :], start=True, stop=True),
                ('pe_rb',))
            pop(lambda t: t.matmul(psEF[:, 512:513], on1S[:, :], tmprow[:, :], start=True, stop=True),
                ('pe_tmp',))
            vop(lambda v: v.tensor_copy(b1T[:, :], psA[:, 0:EXP * FE]), ('v_b1T',), waits=[('pe_b1T',)])
            vop(lambda v: v.tensor_copy(fb1T[:, :], psB[:, 0:32]), ('v_fb1T',), waits=[('pe_fb1T',)])
            vop(lambda v: v.tensor_copy(lngrep[:, :], psCD[:, 0:HR]), ('v_lng',), waits=[('pe_lng',)])
            vop(lambda v: v.tensor_copy(lnbrep[:, :], psCD[:, 512:512 + HR]), ('v_lnb',), waits=[('pe_lnb',)])
            vop(lambda v: v.tensor_copy(rbrep[:, :], psEF[:, 0:EXP]), ('v_rb',), waits=[('pe_rb',)])
            vop(lambda v: v.tensor_copy(tmpr[:, :], psEF[:, 512:513]), ('v_tmp',), waits=[('pe_tmp',)])
            vop(lambda v: v.reciprocal(invt[:, :], tmpr[:, :]), ('v_invt',))
            bset(psA, ('v_b1T',)); bset(psB, ('v_fb1T',))
            bset(psCD, ('v_lnb',)); bset(psEF, ('v_tmp',))

        xn_last = [None, None, None, None]
        hg_last = [None, None, None, None]
        UPB = [psA, psB, psH]
        upbank = {id(psA): None, id(psB): None, id(psH): None}
        upcnt = [0]

        def oview(c, i, lo=0, hi=D):
            par = c % 3
            return oac[:, (par * TT + i) * D + lo:(par * TT + i) * D + hi]

        def produce_stages(cc):
            par = cc % 2
            xo = par * KD * CT
            po = par * TT * EXP
            to = par * CT

            def xn_dmaT():
                if cc >= 2:
                    env.wait('sync', ('pe_up_all', cc - 2, LAST_P))
                for i in range(TT):
                    env.wait('sync', ('v_xn', cc, i))
                    for j in range(KD):
                        ins = (SY.dma_start_transpose(
                            xnT[:, xo + j * CT + i * 128:xo + j * CT + (i + 1) * 128],
                            xnS[:, i * D + j * 128:i * D + (j + 1) * 128])
                            if E('sync') else None)
                        key = ('dt_i', cc, i) if j == KD - 1 else None
                        env.mark('dt%d' % i, key, ins, 16)

            def hmm_pe(i):
                bk = psG
                bwait(bk)
                w = [('dt_i', cc, i)]
                if cc == 0 and i == 0:
                    w += [('gs_all',), ('lx_abc',)]
                for kt in range(KD):
                    pop(lambda t, k=kt, b=bk: t.matmul(
                        b[:, 0:HR], xnT[:, xo + k * CT + i * 128:xo + k * CT + (i + 1) * 128],
                        r1buf[:, k * HR:(k + 1) * HR], start=(k == 0), stop=False),
                        None, waits=(w if kt == 0 else []))
                pop(lambda t, b=bk: t.matmul(b[:, 0:HR], on1bS[:, :], r1brow[:, :],
                                             start=False, stop=True), ('pe_h', cc, i))

            def hmm_v(i):
                bk = psG
                w = [('pe_h', cc, i)]
                if i == 0 and cc >= 1:
                    w.append(('v_aff', cc - 1, TT - 1))
                sop(lambda s, b=bk: s.activation(hcS[:, i * HR:(i + 1) * HR], b[:, 0:HR],
                                                 AF.Copy, accum_out=mu2S[:, i:i + 1]),
                    ('s_hev', cc, i), waits=w)
                bset(bk, ('s_hev', cc, i))
                sop(lambda s: s.activation(hgS[:, D:D + HR], hcS[:, i * HR:(i + 1) * HR],
                                           AF.Square, accum_out=var2S[:, i:i + 1]),
                    ('s_hsq', cc, i))

            def hg_vs(i):
                sl = i * HR
                w = [('s_rstd2', cc)] + ([hg_last[i]] if hg_last[i] else [])
                vop(lambda v: v.tensor_scalar(hgS[:, sl:sl + HR],
                                              hcS[:, i * HR:(i + 1) * HR],
                                              mu2S[:, i:i + 1], rstd2S[:, i:i + 1],
                                              op0=ALU.subtract, op1=ALU.mult),
                    None, waits=w)
                vop(lambda v: v.tensor_mul(hgS[:, sl:sl + HR], hgS[:, sl:sl + HR],
                                           lngrep[:, :]))
                vop(lambda v: v.tensor_add(hgS[:, sl:sl + HR], hgS[:, sl:sl + HR],
                                           lnbrep[:, :]), ('v_aff', cc, i))
                sop(lambda s: s.activation(hgG[:, sl:sl + HR], hgS[:, sl:sl + HR], AF.Gelu),
                    ('s_hg', cc, i), waits=[('v_aff', cc, i)])

            def hg_dmaT():
                if cc >= 1:
                    env.wait('sync', ('pe_lg', cc - 1))
                for i in range(TT):
                    env.wait('sync', ('s_hg', cc, i))
                    for j in range(KH):
                        ins = (SY.dma_start_transpose(
                            hgT[:, j * CT + i * 128:j * CT + (i + 1) * 128],
                            hgG[:, i * HR + j * 128:i * HR + (j + 1) * 128])
                            if E('sync') else None)
                        key = ('dh_i', cc, i) if j == KH - 1 else None
                        env.mark('dh%d' % i, key, ins, 16)
                    hg_last[i] = ('pe_lg', cc)

            def pre_v_piece(i):
                w = [('dx', cc)] if i == 0 else []
                if i == 0 and cc >= 1:
                    w.append(('v_xn', cc - 1, TT - 1))
                sop(lambda s, ii=i: s.activation(hgS[:, 0:D], oview(cc, ii), AF.Copy,
                                                 accum_out=musS[:, ii:ii + 1]),
                    None, waits=w)
                sop(lambda s, ii=i: s.activation(hgS[:, 0:D], oview(cc, ii), AF.Square,
                                                 accum_out=varS[:, ii:ii + 1]),
                    ('s_acc', cc) if i == TT - 1 else None)
                if i == TT - 1:
                    vop(lambda v: v.tensor_scalar(musS[:, :], musS[:, :], 1.0 / D, None,
                                                  op0=ALU.mult), None, waits=[('s_acc', cc)])
                    vop(lambda v: v.tensor_mul(msqS[:, :], musS[:, :], musS[:, :]))
                    vop(lambda v: v.tensor_scalar(varS[:, :], varS[:, :], 1.0 / D, LN_EPS,
                                                  op0=ALU.mult, op1=ALU.add))
                    vop(lambda v: v.tensor_sub(varS[:, :], varS[:, :], msqS[:, :]),
                        ('v_var', cc))
                    sop(lambda s: s.activation(rstdS[:, :], varS[:, :], AF.Ln), None,
                        waits=[('v_var', cc)])
                    sop(lambda s: s.activation(rstdS[:, :], rstdS[:, :], AF.Exp, scale=-0.5),
                        ('s_rstd', cc))

            def s0():
                for i in range(TT):
                    w = [('s_rstd', cc)]
                    if i == 0 and cc >= 1:
                        w.extend(('dt_i', cc - 1, t) for t in range(TT))
                    if i == 0 and cc >= 3:
                        w.append(('pe_up_all', cc - 1, 0))
                    vop(lambda v, ii=i: v.tensor_scalar(xnS[:, ii * D:(ii + 1) * D],
                                                        oview(cc, ii),
                                                        musS[:, ii:ii + 1], rstdS[:, ii:ii + 1],
                                                        op0=ALU.subtract, op1=ALU.mult),
                        ('v_xn', cc, i), waits=w)
                xn_dmaT()

            def rstd2_batch():
                vop(lambda v: v.tensor_scalar(mu2S[:, :], mu2S[:, :], 1.0 / HR, None,
                                              op0=ALU.mult), None, waits=[('s_hsq', cc, TT - 1)])
                vop(lambda v: v.tensor_mul(msq2S[:, :], mu2S[:, :], mu2S[:, :]))
                vop(lambda v: v.tensor_scalar(var2S[:, :], var2S[:, :], 1.0 / HR, LN_EPS,
                                              op0=ALU.mult, op1=ALU.add))
                vop(lambda v: v.tensor_sub(var2S[:, :], var2S[:, :], msq2S[:, :]),
                    ('v_var2', cc))
                sop(lambda s: s.activation(rstd2S[:, :], var2S[:, :], AF.Ln), None,
                    waits=[('v_var2', cc)])
                sop(lambda s: s.activation(rstd2S[:, :], rstd2S[:, :], AF.Exp, scale=-0.5),
                    ('s_rstd2', cc))

            def s7():
                bwait(psG)
                for i in range(TT):
                    w = [('dh_i', cc, i)]
                    for kt in range(KH):
                        pop(lambda t, k=kt, ii=i: t.matmul(
                            psG[:, ii * EXP:(ii + 1) * EXP],
                            hgT[:, k * CT + ii * 128:k * CT + (ii + 1) * 128],
                            r2buf[:, k * EXP:(k + 1) * EXP], start=(k == 0), stop=False),
                            None, waits=(w if kt == 0 else []))
                    pop(lambda t, ii=i: t.matmul(psG[:, ii * EXP:(ii + 1) * EXP], on1bS[:, :],
                                                 r2brow[:, :], start=False, stop=True),
                        ('pe_lg', cc) if i == TT - 1 else None)
                for i in range(TT):
                    g = slice(i * EXP, (i + 1) * EXP)
                    vop(lambda v, gg=g: v.tensor_scalar(lgS[:, gg], psG[:, gg], invt[:, 0:1],
                                                        None, op0=ALU.mult),
                        ('v_lge', cc) if i == TT - 1 else None,
                        waits=[('pe_lg', cc)] if i == 0 else [])
                    vop(lambda v, gg=g: v.tensor_add(lgS[:, gg], lgS[:, gg], rbrep[:, :]))
                    vop(lambda v, gg=g, ii=i: v.reduce_max(nmxS[:, ii:ii + 1], lgS[:, gg],
                                                           axis=AX.X, negate=True),
                        ('v_nmx', cc, i))
                bset(psG, ('v_lge', cc))

            def s8a():
                for i in range(TT):
                    g = slice(i * EXP, (i + 1) * EXP)
                    sop(lambda s, gg=g, ii=i: s.activation(expS[:, gg], lgS[:, gg], AF.Exp,
                                                           bias=nmxS[:, ii:ii + 1]),
                        ('s_exp', cc, i), waits=[('v_nmx', cc, TT - 1)] if i == 0 else [])

            def s8a_piece(i):
                if True:
                    g = slice(i * EXP, (i + 1) * EXP)
                    pg = slice(po + i * EXP, po + (i + 1) * EXP)
                    vop(lambda v, gg=g, ii=i: v.reduce_sum(seS[:, ii:ii + 1], expS[:, gg],
                                                           axis=AX.X),
                        None, waits=[('s_exp', cc, i)])
                    vop(lambda v, ii=i: v.reciprocal(rseS[:, ii:ii + 1], seS[:, ii:ii + 1]))
                    vop(lambda v, gg=g, pgg=pg, ii=i: v.tensor_scalar(prS[:, pgg], expS[:, gg],
                                                                      rseS[:, ii:ii + 1],
                                                                      None, op0=ALU.mult))
                    vop(lambda v, pgg=pg, ii=i: v.reduce_max(t1S[:, ii:ii + 1], prS[:, pgg],
                                                             axis=AX.X))
                    vop(lambda v, pgg=pg, ii=i: v.tensor_scalar(eqbS[:, :], prS[:, pgg],
                                                                t1S[:, ii:ii + 1], BIG,
                                                                op0=ALU.is_equal, op1=ALU.mult))
                    vop(lambda v, pgg=pg: v.tensor_sub(pmS[:, :], prS[:, pgg], eqbS[:, :]))
                    vop(lambda v, ii=i: v.reduce_max(t2S[:, ii:ii + 1], pmS[:, :], axis=AX.X))
                    vop(lambda v, pgg=pg, ii=i: v.scalar_tensor_tensor(prS[:, pgg], prS[:, pgg],
                                                                       t2S[:, ii:ii + 1],
                                                                       prS[:, pgg],
                                                                       op0=ALU.is_ge,
                                                                       op1=ALU.mult),
                        ('v_pr', cc, i))

            def s8b():
                bwait(psG)
                for i in range(TT):
                    pop(lambda t, ii=i: t.transpose(psG[0:EXP, ii * 128:(ii + 1) * 128],
                                                    prS[:, po + ii * EXP:po + (ii + 1) * EXP],
                                                    idS[:, :]),
                        ('pe_pT', cc) if i == TT - 1 else None, waits=[('v_pr', cc, i)])
                vop(lambda v: v.tensor_copy(pTbf[:, to:to + CT], psG[0:EXP, 0:CT]),
                    ('v_pTe', cc), waits=[('pe_pT', cc)])
                bset(psG, ('v_pTe', cc))

            pre = [None] * 10
            mid = [None] * 10
            vp = [None] * 10
            vp[0] = [lambda ii=i: pre_v_piece(ii) for i in range(TT)]
            pre[1] = s0
            pre[2] = lambda: hmm_pe(0)
            mid[2] = lambda: hmm_v(0)
            pre[3] = lambda: hmm_pe(1)
            mid[3] = lambda: hmm_v(1)
            pre[4] = lambda: hmm_pe(2)
            mid[4] = lambda: hmm_v(2)
            pre[5] = lambda: hmm_pe(3)
            mid[5] = lambda: hmm_v(3)
            vp[5] = [rstd2_batch]
            vp[6] = [lambda ii=i: hg_vs(ii) for i in range(TT)]
            pre[7] = hg_dmaT
            pre[8] = s7
            mid[8] = lambda: (s8a(), s8a_piece(0), s8a_piece(1))
            vp[8] = [lambda: s8a_piece(2), lambda: s8a_piece(3)]
            pre[9] = s8b
            return pre, mid, vp

        def consume(c, produce_next):
            par = c % 2
            xo = par * KD * CT
            po = par * TT * EXP
            to = par * CT

            if c >= 1:
                pp = (c - 1) % 3
                env.wait('sync', ('v_dnall', c - 1))
                ins = (SY.dma_start(
                    dY[(c - 1) * CT:c * CT, :].rearrange("(tt p) d -> p tt d", p=128),
                    oac[:, pp * TT * D:(pp + 1) * TT * D].rearrange("p (tt d) -> p tt d", tt=TT))
                    if E('sync') else None)
                env.mark('do', ('do', c - 1), ins, 16)
            if c + 2 < NCH:
                cn = c + 2
                pn = cn % 3
                env.wait('sync', ('dx', cn - 1))
                if cn >= 3:
                    env.wait('sync', ('do', cn - 3))
                ins = (SY.dma_start(
                    oac[:, pn * TT * D:(pn + 1) * TT * D].rearrange("p (tt d) -> p tt d", tt=TT),
                    dx_[cn * CT:(cn + 1) * CT, :].rearrange("(tt p) d -> p tt d", p=128))
                    if E('sync') else None)
                env.mark('dx', ('dx', cn), ins, 16)

            for pi, (kind, idx) in enumerate(PX):
                prev = (c, pi - 1) if pi > 0 else ((c - 1, LAST_P) if c > 0 else None)
                nf = FE if kind == 'e' else FQ
                fw = 1536 if kind == 'e' else 1024
                if c == 0:
                    if prev is not None:
                        env.wait('gp', ('pe_up_all',) + prev)
                    env.wait('gp', ('st1', pi))
                    dst1 = w1buf[:, 0:KD * fw].rearrange("p (kt f) -> p kt f", kt=KD)
                    ins = G.dma_start(dst1, w1_src(kind, idx)) if E('gp') else None
                    env.mark('w1g', ('gw1', c, pi), ins, 16)
                if c <= 1:
                    gseq = c * NPX + pi
                    if gseq >= 2:
                        p2 = gseq - 2
                        env.wait('gp', ('pe_dn_all', p2 // NPX, p2 % NPX))
                    if c == 1 and pi >= 2:
                        env.wait('gp', ('st2s', pi - 2))
                    w2o = (gseq % 2) * FE * D
                    dst2 = w2buf[:, w2o:w2o + nf * D].rearrange("p (fk d) -> p fk d", fk=nf)
                    ins = G.dma_start(dst2, w2_src(kind, idx)) if E('gp') else None
                    env.mark('w2g' if gseq % 2 == 0 else 'w3g', ('gw2', c, pi), ins, 16)
                if c == 0:
                    env.wait('gp', ('gw1', c, pi))
                    ins = G.dma_start(scr1[pi, :, 0:KD * fw], w1buf[:, 0:KD * fw]) if E('gp') else None
                    env.mark('s1', ('st1', pi + 1), ins, 16)

            HKT = 2
            HC = HKT * 1536

            def emit_gw1h(cw):
                for t in range(TT):
                    env.wait('s', ('dt_i', cw, t))
                ins = S.dma_start(
                    xnS[:, 0:HC].rearrange("p (kt f) -> p kt f", kt=HKT),
                    scr1[0, :, 0:HC]) if E('s') else None
                env.mark('w1h', ('gw1h', cw), ins, 16)

            def emit_gw1(cw, pw):
                prev = (cw, pw - 1) if pw > 0 else (cw - 1, LAST_P)
                env.wait('s', ('pe_up_all',) + prev)
                if cw == 1:
                    env.wait('s', ('st1', NPX))
                fwv = 1536 if PX[pw][0] == 'e' else 1024
                lo = HC if (pw == 0 and cw >= 2) else 0
                dst1 = w1buf[:, lo:KD * fwv].rearrange("p (kt f) -> p kt f", kt=KD - (HKT if lo else 0))
                ins = S.dma_start(dst1, scr1[pw, :, lo:KD * fwv]) if E('s') else None
                env.mark('w1', ('gw1', cw, pw), ins, 16)

            def emit_scr2_store(pw):
                nfw = FE if PX[pw][0] == 'e' else FQ
                w2o = ((NPX + pw) % 2) * FE * D
                env.wait('sync', ('gw2', 1, pw))
                if pw >= 1:
                    env.wait('sync', ('st2s', pw - 1))
                ins = SY.dma_start(scr2[pw, :, 0:nfw * D], w2buf[:, w2o:w2o + nfw * D]) if E('sync') else None
                env.mark('s2', ('st2s', pw), ins, 16)

            def emit_gw2(cw, pw):
                gseq = cw * NPX + pw
                p2 = gseq - 2
                env.wait('s', ('pe_dn_all', p2 // NPX, p2 % NPX))
                if cw == 2:
                    env.wait('s', ('st2s', LAST_P))
                nfw = FE if PX[pw][0] == 'e' else FQ
                w2o = (gseq % 2) * FE * D
                dst2 = w2buf[:, w2o:w2o + nfw * D].rearrange("p (fk d) -> p fk d", fk=nfw)
                ins = S.dma_start(dst2, scr2[pw, :, 0:nfw * D]) if E('s') else None
                env.mark('w2' if gseq % 2 == 0 else 'w3', ('gw2', cw, pw), ins, 16)

            for pi, (kind, idx) in enumerate(PX):
                if produce_next is not None and produce_next[0][pi] is not None:
                    produce_next[0][pi]()
                nf = FE if kind == 'e' else FQ
                fw = 1536 if kind == 'e' else 1024
                for f in range(nf):
                    bk = UPB[upcnt[0] % 3]
                    upcnt[0] += 1
                    w = []
                    if f == 0:
                        w.append(('gw1', c, pi))
                        if pi == 0:
                            w.extend(('dt_i', c, t) for t in range(TT))
                            if c >= 2:
                                w.append(('gw1h', c))
                    if upbank[id(bk)] is not None:
                        w.append(upbank[id(bk)])
                    for kt in range(KD):
                        key = ('pe_up', c, pi, f) if kt == KD - 1 else None
                        wsrc = xnS if (c >= 2 and pi == 0 and kt < 2) else w1buf
                        pop(lambda t, k=kt, b=bk, ff=f, ws=wsrc: t.matmul(
                            b[:, :], ws[:, k * fw + ff * 128:k * fw + (ff + 1) * 128],
                            xnT[:, xo + k * CT:xo + (k + 1) * CT], start=(k == 0),
                            stop=(k == KD - 1)),
                            key, waits=(w if kt == 0 else []))
                    if kind == 'e':
                        af = AF.Gelu if idx % 2 == 0 else AF.Silu
                        sop(lambda s, b=bk, ff=f: s.activation(
                            H_sb[:, ff * CT:(ff + 1) * CT], b[:, :], af,
                            bias=b1T[:, idx * FE + ff:idx * FE + ff + 1]),
                            ('s_ue', c, pi, f),
                            waits=[('pe_up', c, pi, f)] + ([('v_b1T',)] if c == 0 and pi == 0 and f == 0 else []))
                        upbank[id(bk)] = ('s_ue', c, pi, f)
                    else:
                        sop(lambda s, b=bk, ff=f: s.activation(
                            H_sb[:, ff * CT:(ff + 1) * CT], b[:, :], AF.Relu,
                            bias=fb1T[:, idx * FQ + ff:idx * FQ + ff + 1]),
                            ('s_ue', c, pi, f),
                            waits=[('pe_up', c, pi, f)] + ([('v_fb1T',)] if c == 0 and idx == 0 and f == 0 else []))
                        upbank[id(bk)] = ('s_ue', c, pi, f)
                env.alias(('pe_up_all', c, pi), ('pe_up', c, pi, nf - 1))
                if produce_next is not None and produce_next[1][pi] is not None:
                    produce_next[1][pi]()
                if pi + 1 < NPX:
                    if c >= 1:
                        emit_gw1(c, pi + 1)
                    if c >= 1 and pi == 2 and c + 1 < NCH:
                        emit_gw1h(c + 1)
                    if c == 1:
                        emit_scr2_store(pi)
                    elif c >= 2:
                        emit_gw2(c, pi + 1)
                elif c + 1 < NCH:
                    emit_gw1(c + 1, 0)
                    if c == 1:
                        emit_scr2_store(LAST_P)
                    if c >= 1:
                        emit_gw2(c + 1, 0)
                def act_mark(fk):
                    return ('s_ue', c, pi, fk)
                w2o = ((c * NPX + pi) % 2) * FE * D
                for i in range(TT):
                    pr = DNP[i % 2]
                    bC = pr[:, 0:512]
                    bD = pr[:, 512:1024]
                    bwait(pr)
                    first = True
                    if kind == 'f' and idx == 0:
                        pop(lambda t, ii=i: t.matmul(bC,
                                                     pTbf[:, to + ii * 128:to + (ii + 1) * 128],
                                                     b2sb[:, 0:512], start=True, stop=False),
                            None, waits=([('v_pTe', c)] if i == 0 else []))
                        pop(lambda t, ii=i: t.matmul(bD,
                                                     pTbf[:, to + ii * 128:to + (ii + 1) * 128],
                                                     b2sb[:, 512:1024], start=True, stop=False))
                        pop(lambda t: t.matmul(bC, on1bS[:, :], fb2row[:, 0:512],
                                               start=False, stop=False))
                        pop(lambda t: t.matmul(bD, on1bS[:, :], fb2row[:, 512:1024],
                                               start=False, stop=False))
                        first = False
                    for fk in range(nf):
                        stv = first and fk == 0
                        spv = fk == nf - 1
                        w = []
                        if i == 0:
                            w.append(act_mark(fk))
                            if fk == 0:
                                w.append(('gw2', c, pi))
                        pop(lambda t, k=fk, s0v=stv, s1v=spv, ii=i: t.matmul(
                            bC, H_sb[:, k * CT + ii * 128:k * CT + (ii + 1) * 128],
                            w2buf[:, w2o + k * D:w2o + k * D + 512], start=s0v, stop=s1v),
                            None, waits=w)
                        key = ('pe_dn', c, pi, i) if spv else None
                        pop(lambda t, k=fk, s0v=stv, s1v=spv, ii=i: t.matmul(
                            bD, H_sb[:, k * CT + ii * 128:k * CT + (ii + 1) * 128],
                            w2buf[:, w2o + k * D + 512:w2o + (k + 1) * D], start=s0v, stop=s1v), key)
                    if kind == 'e':
                        pc = prS[:, po + i * EXP + idx:po + i * EXP + idx + 1]
                        vop(lambda v, ii=i, p_=pc, b=pr: v.scalar_tensor_tensor(
                            oview(c, ii), b[:, :], p_, oview(c, ii),
                            op0=ALU.mult, op1=ALU.add),
                            ('v_dn', c, pi, i), waits=[('pe_dn', c, pi, i)])
                    else:
                        vop(lambda v, ii=i, b=pr: v.tensor_add(oview(c, ii),
                                                               oview(c, ii), b[:, :]),
                            ('v_dn', c, pi, i), waits=[('pe_dn', c, pi, i)])
                    bset(pr, ('v_dn', c, pi, i))
                    if produce_next is not None:
                        vps = produce_next[2][pi]
                        if vps is not None and i < len(vps):
                            vps[i]()
                env.alias(('pe_dn_all', c, pi), ('pe_dn', c, pi, TT - 1))
            env.alias(('v_dnall', c), ('v_dn', c, LAST_P, TT - 1))

        ins = (SY.dma_start(
            oac[:, 0:TT * D].rearrange("p (tt d) -> p tt d", tt=TT),
            dx_[0:CT, :].rearrange("(tt p) d -> p tt d", p=128))
            if E('sync') else None)
        env.mark('dx', ('dx', 0), ins, 16)
        env.wait('sync', ('dx', 0))
        ins = (SY.dma_start(
            oac[:, TT * D:2 * TT * D].rearrange("p (tt d) -> p tt d", tt=TT),
            dx_[CT:2 * CT, :].rearrange("(tt p) d -> p tt d", p=128))
            if E('sync') else None)
        env.mark('dx', ('dx', 1), ins, 16)

        pre0, mid0, vp0 = produce_stages(0)
        for p in vp0[0]:
            p()
        for j in range(10):
            if pre0[j] is not None:
                pre0[j]()
            if mid0[j] is not None:
                mid0[j]()
            if j >= 1 and vp0[j] is not None:
                for p in vp0[j]:
                    p()
            if j == 5:
                setup_compute()
        for c in range(NCH):
            produce_next = produce_stages(c + 1) if c + 1 < NCH else None
            consume(c, produce_next)

        c = NCH - 1
        pp = c % 3
        env.wait('sync', ('v_dnall', c))
        ins = (SY.dma_start(
            dY[c * CT:(c + 1) * CT, :].rearrange("(tt p) d -> p tt d", p=128),
            oac[:, pp * TT * D:(pp + 1) * TT * D].rearrange("p (tt d) -> p tt d", tt=TT))
            if E('sync') else None)
        env.mark('do', ('do', c), ins, 16)
        env.wait('sync', ('do', c))

    return nc, st, sems, emit


def build():
    nc, st, SEMS, emit = build_nc()
    marks = {}
    emit(Env(None, marks))
    with st:
        with nc.Block() as blk:
            def run(eng_name):
                def body(engine):
                    env = Env(eng_name, marks, SEMS, {eng_name: engine})
                    emit(env)
                return body
            blk.tensor(run('pe'))
            blk.vector(run('v'))
            blk.scalar(run('s'))
            blk.sync(run('sync'))
            blk.gpsimd(run('gp'))
    return nc


_NC_CACHE = None


def kernel(**inputs):
    global _NC_CACHE
    gi = lambda k: np.ascontiguousarray(np.asarray(inputs[k], dtype=np.float32))
    x = gi("L0_attn_x").reshape(NCORE * TOK, D)
    shards = np.split(x, NCORE, axis=0)

    common = dict(
        aW1=gi("L0_attn_W1"), aB1=gi("L0_attn_b1").reshape(EXP * FE, 128),
        aW2=gi("L0_attn_W2"), aB2=gi("L0_attn_b2"),
        r1w=gi("L0_attn_r1_w"), r1b=gi("L0_attn_r1_b").reshape(1, HR),
        lng=gi("L0_attn_ln_g").reshape(1, HR), lnb=gi("L0_attn_ln_b").reshape(1, HR),
        r2w=gi("L0_attn_r2_w"), r2b=gi("L0_attn_r2_b").reshape(1, EXP),
        temp=gi("L0_attn_temp").reshape(1, 1), rbias=gi("L0_attn_route_bias").reshape(1, EXP),
        fW1=gi("L0_ffn_W1"), fB1=gi("L0_ffn_b1").reshape(32, 128),
        fW2=gi("L0_ffn_W2"), fB2=gi("L0_ffn_b2").reshape(1, D),
        ident=np.eye(128, dtype=np.float32),
        ones1=np.ones((1, 128), dtype=np.float32),
        ones1b=np.ones((1, 128), dtype=ml_dtypes.bfloat16),
    )
    if _NC_CACHE is None:
        _NC_CACHE = build()
    nc = _NC_CACHE
    in_maps = [dict(common, x=shards[ci]) for ci in range(NCORE)]
    res = run_bass_kernel_spmd(nc, in_maps, core_ids=list(range(NCORE)))
    y = np.concatenate([r["y"] for r in res.results], axis=0)
    return y.reshape(NCORE, TOK, D).astype(np.float32, copy=False)


# version 51
# speedup vs baseline: 1.1867x; 1.0066x over previous
import numpy as np
import ml_dtypes
from contextlib import ExitStack

import concourse.bass as bass
import concourse.mybir as mybir
from concourse.bass_utils import run_bass_kernel_spmd

F32 = mybir.dt.float32
BF16 = mybir.dt.bfloat16
AF = mybir.ActivationFunctionType
ALU = mybir.AluOpType
AX = mybir.AxisListType

NCORE = 8
TOK = 4096
D = 1024
CT = 512
NCH = TOK // CT
TT = CT // 128
KD = D // 128
EXP = 6
FE = 12
FQ = 8
HR = 512
KH = HR // 128
LN_EPS = 1e-5
BIG = 3.0e38

PX = [('e', 0), ('e', 2), ('e', 4), ('f', 0), ('f', 1), ('f', 2), ('f', 3),
      ('e', 1), ('e', 3), ('e', 5)]
PX0 = [('f', 1), ('f', 2), ('e', 0), ('e', 2), ('e', 4), ('f', 0), ('f', 3),
       ('e', 1), ('e', 3), ('e', 5)]
LAST_P = len(PX) - 1
NPX = len(PX)


class Env:
    def __init__(self, eng, marks, sems=None, engines=None):
        self.eng = eng
        self.marks = marks
        self.sems = sems or {}
        self.engines = engines or {}
        self.counts = {k: 0 for k in
                       ('pe', 'v', 's', 'w1', 'w1g', 'w1h', 'w2', 'w3', 'w2g', 'w3g', 'gs', 'lx', 'dx', 'do', 's1', 's2',
                        'dt0', 'dt1', 'dt2', 'dt3', 'dh0', 'dh1', 'dh2', 'dh3')}

    def mark(self, sem, key, ins=None, d=1):
        self.counts[sem] += d
        if self.eng is None:
            if key is not None:
                self.marks[key] = (sem, self.counts[sem])
        elif ins is not None:
            ins.then_inc(self.sems[sem], d)
        return self.counts[sem]

    def alias(self, new_key, old_key):
        if self.eng is None:
            self.marks[new_key] = self.marks[old_key]

    def wait(self, eng, key):
        if self.eng == eng and key is not None:
            sem, val = self.marks[key]
            if val > 0:
                self.engines[eng].wait_ge(self.sems[sem], val)

    def wait_prev(self, eng):
        if self.eng == eng and self.counts[eng] > 0:
            self.engines[eng].wait_ge(self.sems[eng], self.counts[eng])


def build_nc():
    nc = bass.Bass("TRN2", target_bir_lowering=False)

    dx_ = nc.dram_tensor("x", [TOK, D], F32, kind="ExternalInput")
    dW1 = nc.dram_tensor("aW1", [EXP, D, 1536], F32, kind="ExternalInput")
    dB1 = nc.dram_tensor("aB1", [EXP * FE, 128], F32, kind="ExternalInput")
    dW2 = nc.dram_tensor("aW2", [EXP, 1536, D], F32, kind="ExternalInput")
    dB2 = nc.dram_tensor("aB2", [EXP, D], F32, kind="ExternalInput")
    dR1 = nc.dram_tensor("r1w", [D, HR], F32, kind="ExternalInput")
    dR1b = nc.dram_tensor("r1b", [1, HR], F32, kind="ExternalInput")
    dLng = nc.dram_tensor("lng", [1, HR], F32, kind="ExternalInput")
    dLnb = nc.dram_tensor("lnb", [1, HR], F32, kind="ExternalInput")
    dR2 = nc.dram_tensor("r2w", [HR, EXP], F32, kind="ExternalInput")
    dR2b = nc.dram_tensor("r2b", [1, EXP], F32, kind="ExternalInput")
    dTmp = nc.dram_tensor("temp", [1, 1], F32, kind="ExternalInput")
    dRb = nc.dram_tensor("rbias", [1, EXP], F32, kind="ExternalInput")
    dF1 = nc.dram_tensor("fW1", [D, 4096], F32, kind="ExternalInput")
    dFb1 = nc.dram_tensor("fB1", [32, 128], F32, kind="ExternalInput")
    dF2 = nc.dram_tensor("fW2", [4096, D], F32, kind="ExternalInput")
    dFb2 = nc.dram_tensor("fB2", [1, D], F32, kind="ExternalInput")
    dId = nc.dram_tensor("ident", [128, 128], F32, kind="ExternalInput")
    dOne = nc.dram_tensor("ones1", [1, 128], F32, kind="ExternalInput")
    dOneB = nc.dram_tensor("ones1b", [1, 128], BF16, kind="ExternalInput")
    dY = nc.dram_tensor("y", [TOK, D], F32, kind="ExternalOutput")
    scr1 = nc.dram_tensor("scr1", [NPX, 128, KD * 1536], BF16, kind="Internal")
    scr2 = nc.dram_tensor("scr2", [NPX, 128, FE * D], BF16, kind="Internal")

    st = ExitStack()
    e = st.enter_context

    oac = e(nc.sbuf_tensor("oac", [128, 3 * TT * D], F32))
    xnT = e(nc.sbuf_tensor("xnT", [128, 2 * KD * CT], BF16))
    xnS = e(nc.sbuf_tensor("xnS", [128, 4 * D], BF16))
    hcS = e(nc.sbuf_tensor("hcS", [128, TT * HR], F32))
    hgS = e(nc.sbuf_tensor("hgS", [128, 4 * HR], F32))
    hgG = e(nc.sbuf_tensor("hgG", [128, 4 * HR], BF16))
    hgT = e(nc.sbuf_tensor("hgT", [128, KH * CT], BF16))
    H_sb = e(nc.sbuf_tensor("H_sb", [128, FE * CT], BF16))
    w1buf = e(nc.sbuf_tensor("w1buf", [128, KD * 1536], BF16))
    w2buf = e(nc.sbuf_tensor("w2buf", [128, 2 * FE * D], BF16))
    r1buf = e(nc.sbuf_tensor("r1buf", [128, KD * HR], BF16))
    r2buf = e(nc.sbuf_tensor("r2buf", [128, KH * EXP], BF16))
    musS = e(nc.sbuf_tensor("musS", [128, TT], F32))
    varS = e(nc.sbuf_tensor("varS", [128, TT], F32))
    msqS = e(nc.sbuf_tensor("msqS", [128, TT], F32))
    rstdS = e(nc.sbuf_tensor("rstdS", [128, TT], F32))
    mu2S = e(nc.sbuf_tensor("mu2S", [128, TT], F32))
    msq2S = e(nc.sbuf_tensor("msq2S", [128, TT], F32))
    var2S = e(nc.sbuf_tensor("var2S", [128, TT], F32))
    rstd2S = e(nc.sbuf_tensor("rstd2S", [128, TT], F32))
    lgS = e(nc.sbuf_tensor("lgS", [128, TT * EXP], F32))
    expS = e(nc.sbuf_tensor("expS", [128, TT * EXP], F32))
    prS = e(nc.sbuf_tensor("prS", [128, 2 * TT * EXP], F32))
    eqbS = e(nc.sbuf_tensor("eqbS", [128, EXP], F32))
    pmS = e(nc.sbuf_tensor("pmS", [128, EXP], F32))
    nmxS = e(nc.sbuf_tensor("nmxS", [128, TT], F32))
    seS = e(nc.sbuf_tensor("seS", [128, TT], F32))
    rseS = e(nc.sbuf_tensor("rseS", [128, TT], F32))
    t1S = e(nc.sbuf_tensor("t1S", [128, TT], F32))
    t2S = e(nc.sbuf_tensor("t2S", [128, TT], F32))
    pTbf = e(nc.sbuf_tensor("pTbf", [6, 2 * CT], BF16))
    b1T = e(nc.sbuf_tensor("b1T", [128, EXP * FE], F32))
    fb1T = e(nc.sbuf_tensor("fb1T", [128, 32], F32))
    b1st = e(nc.sbuf_tensor("b1st", [EXP * FE, 128], F32))
    fb1st = e(nc.sbuf_tensor("fb1st", [32, 128], F32))
    b2sb = e(nc.sbuf_tensor("b2sb", [EXP, D], BF16))
    fb2row = e(nc.sbuf_tensor("fb2row", [1, D], BF16))
    r1brow = e(nc.sbuf_tensor("r1brow", [1, HR], BF16))
    r2brow = e(nc.sbuf_tensor("r2brow", [1, EXP], BF16))
    lngrep = e(nc.sbuf_tensor("lngrep", [128, HR], F32))
    lnbrep = e(nc.sbuf_tensor("lnbrep", [128, HR], F32))
    rbrep = e(nc.sbuf_tensor("rbrep", [128, EXP], F32))
    invt = e(nc.sbuf_tensor("invt", [128, 1], F32))
    tmpr = e(nc.sbuf_tensor("tmpr", [128, 1], F32))
    lngrow = e(nc.sbuf_tensor("lngrow", [1, HR], F32))
    lnbrow = e(nc.sbuf_tensor("lnbrow", [1, HR], F32))
    rbrow = e(nc.sbuf_tensor("rbrow", [1, EXP], F32))
    tmprow = e(nc.sbuf_tensor("tmprow", [1, 1], F32))
    idS = e(nc.sbuf_tensor("idS", [128, 128], F32))
    on1S = e(nc.sbuf_tensor("on1S", [1, 128], F32))
    on1bS = e(nc.sbuf_tensor("on1bS", [1, 128], BF16))

    psA = e(nc.psum_tensor("psA", [128, 512], F32))
    psB = e(nc.psum_tensor("psB", [128, 512], F32))
    psCD = e(nc.psum_tensor("psCD", [128, 1024], F32))
    psEF = e(nc.psum_tensor("psEF", [128, 1024], F32))
    psG = e(nc.psum_tensor("psG", [128, 512], F32))
    psH = e(nc.psum_tensor("psH", [128, 512], F32))
    DNP = [psCD, psEF]

    sems = {k: e(nc.semaphore("sem_" + k)) for k in
            ('pe', 'v', 's', 'w1', 'w1g', 'w1h', 'w2', 'w3', 'w2g', 'w3g', 'gs', 'lx', 'dx', 'do', 's1', 's2',
                        'dt0', 'dt1', 'dt2', 'dt3', 'dh0', 'dh1', 'dh2', 'dh3')}

    def w1_src(kind, idx):
        if kind == 'e':
            return dW1[idx, :, :].rearrange("(kt p) f -> p kt f", p=128)
        return dF1.rearrange("(kt p) f -> p kt f", p=128)[:, :, idx * 1024:(idx + 1) * 1024]

    def w2_src(kind, idx):
        if kind == 'e':
            return dW2[idx, :, :].rearrange("(fk p) d -> p fk d", p=128)
        return dF2.rearrange("(fk p) d -> p fk d", p=128)[:, idx * FQ:(idx + 1) * FQ, :]

    def emit(env):
        def E(en):
            return env.eng == en
        T = env.engines.get('pe')
        V = env.engines.get('v')
        S = env.engines.get('s')
        SY = env.engines.get('sync')
        G = env.engines.get('gp')

        def vop(fn, key=None, waits=()):
            for w in waits:
                env.wait('v', w)
            env.wait_prev('v')
            ins = fn(V) if E('v') else None
            return env.mark('v', key, ins)

        def sop(fn, key=None, waits=()):
            for w in waits:
                env.wait('s', w)
            env.wait_prev('s')
            ins = fn(S) if E('s') else None
            return env.mark('s', key, ins)

        def pop(fn, key=None, waits=()):
            for w in waits:
                env.wait('pe', w)
            ins = fn(T) if E('pe') else None
            return env.mark('pe', key, ins)

        bank_free = {id(b): None for b in (psA, psB, psCD, psEF, psG, psH)}

        def bwait(bank):
            env.wait('pe', bank_free[id(bank)])

        def bset(bank, key):
            bank_free[id(bank)] = key

        for dst, src in [
            (idS[:, :], dId[:, :]), (b1st[:, :], dB1[:, :]), (fb1st[:, :], dFb1[:, :]),
            (on1bS[:, :], dOneB[:, :]),
        ]:
            ins = S.dma_start(dst, src) if E('s') else None
            env.mark('lx', None, ins, 16)
        if env.eng is None:
            env.marks[('lx_abc',)] = ('lx', env.counts['lx'])
        env.wait('s', ('lx_abc',))
        for dst, src in [
            (on1S[:, :], dOne[:, :]),
            (lngrow[:, :], dLng[:, :]), (lnbrow[:, :], dLnb[:, :]),
            (rbrow[:, :], dRb[:, :]), (tmprow[:, :], dTmp[:, :]),
        ]:
            ins = S.dma_start(dst, src) if E('s') else None
            env.mark('lx', None, ins, 16)
        if env.eng is None:
            env.marks[('lx_all',)] = ('lx', env.counts['lx'])
        for dst, src in [
            (r1buf[:, :].rearrange("p (kt h) -> p kt h", kt=KD),
             dR1.rearrange("(kt p) h -> p kt h", p=128)),
            (r2buf[:, :].rearrange("p (kt s) -> p kt s", kt=KH),
             dR2.rearrange("(kt p) s -> p kt s", p=128)),
            (b2sb[:, :], dB2[:, :]),
            (fb2row[:, :], dFb2[:, :]),
            (r1brow[:, :], dR1b[:, :]),
            (r2brow[:, :], dR2b[:, :]),
        ]:
            ins = G.dma_start(dst, src) if E('gp') else None
            env.mark('gs', None, ins, 16)
        if env.eng is None:
            env.marks[('gs_all',)] = ('gs', env.counts['gs'])
            env.marks[('st1', 0)] = ('s1', 0)
            env.marks[('st2', 0)] = ('s2', 0)
        env.wait('gp', ('gs_all',))

        def setup_compute():
            pop(lambda t: t.transpose(psA[:, 0:EXP * FE], b1st[:, :], idS[0:EXP * FE, 0:EXP * FE]),
                ('pe_b1T',), waits=[('lx_abc',)])
            pop(lambda t: t.transpose(psB[:, 0:32], fb1st[:, :], idS[0:32, 0:32]),
                ('pe_fb1T',))
            pop(lambda t: t.matmul(psCD[:, 0:HR], on1S[:, :], lngrow[:, :], start=True, stop=True),
                ('pe_lng',), waits=[('lx_all',)])
            pop(lambda t: t.matmul(psCD[:, 512:512 + HR], on1S[:, :], lnbrow[:, :], start=True, stop=True),
                ('pe_lnb',))
            pop(lambda t: t.matmul(psEF[:, 0:EXP], on1S[:, :], rbrow[:, :], start=True, stop=True),
                ('pe_rb',))
            pop(lambda t: t.matmul(psEF[:, 512:513], on1S[:, :], tmprow[:, :], start=True, stop=True),
                ('pe_tmp',))
            vop(lambda v: v.tensor_copy(b1T[:, :], psA[:, 0:EXP * FE]), ('v_b1T',), waits=[('pe_b1T',)])
            vop(lambda v: v.tensor_copy(fb1T[:, :], psB[:, 0:32]), ('v_fb1T',), waits=[('pe_fb1T',)])
            vop(lambda v: v.tensor_copy(lngrep[:, :], psCD[:, 0:HR]), ('v_lng',), waits=[('pe_lng',)])
            vop(lambda v: v.tensor_copy(lnbrep[:, :], psCD[:, 512:512 + HR]), ('v_lnb',), waits=[('pe_lnb',)])
            vop(lambda v: v.tensor_copy(rbrep[:, :], psEF[:, 0:EXP]), ('v_rb',), waits=[('pe_rb',)])
            vop(lambda v: v.tensor_copy(tmpr[:, :], psEF[:, 512:513]), ('v_tmp',), waits=[('pe_tmp',)])
            vop(lambda v: v.reciprocal(invt[:, :], tmpr[:, :]), ('v_invt',))
            bset(psA, ('v_b1T',)); bset(psB, ('v_fb1T',))
            bset(psCD, ('v_lnb',)); bset(psEF, ('v_tmp',))

        xn_last = [None, None, None, None]
        hg_last = [None, None, None, None]
        UPB = [psA, psB, psH]
        upbank = {id(psA): None, id(psB): None, id(psH): None}
        upcnt = [0]

        def oview(c, i, lo=0, hi=D):
            par = c % 3
            return oac[:, (par * TT + i) * D + lo:(par * TT + i) * D + hi]

        def produce_stages(cc):
            par = cc % 2
            xo = par * KD * CT
            po = par * TT * EXP
            to = par * CT

            def xn_dmaT():
                if cc >= 2:
                    env.wait('sync', ('pe_up_all', cc - 2, LAST_P))
                for i in range(TT):
                    env.wait('sync', ('v_xn', cc, i))
                    for j in range(KD):
                        ins = (SY.dma_start_transpose(
                            xnT[:, xo + j * CT + i * 128:xo + j * CT + (i + 1) * 128],
                            xnS[:, i * D + j * 128:i * D + (j + 1) * 128])
                            if E('sync') else None)
                        key = ('dt_i', cc, i) if j == KD - 1 else None
                        env.mark('dt%d' % i, key, ins, 16)

            def hmm_pe(i):
                bk = psG
                bwait(bk)
                w = [('dt_i', cc, i)]
                if cc == 0 and i == 0:
                    w += [('gs_all',), ('lx_abc',)]
                for kt in range(KD):
                    pop(lambda t, k=kt, b=bk: t.matmul(
                        b[:, 0:HR], xnT[:, xo + k * CT + i * 128:xo + k * CT + (i + 1) * 128],
                        r1buf[:, k * HR:(k + 1) * HR], start=(k == 0), stop=False),
                        None, waits=(w if kt == 0 else []))
                pop(lambda t, b=bk: t.matmul(b[:, 0:HR], on1bS[:, :], r1brow[:, :],
                                             start=False, stop=True), ('pe_h', cc, i))

            def hmm_v(i):
                bk = psG
                w = [('pe_h', cc, i)]
                if i == 0 and cc >= 1:
                    w.append(('v_aff', cc - 1, TT - 1))
                sop(lambda s, b=bk: s.activation(hcS[:, i * HR:(i + 1) * HR], b[:, 0:HR],
                                                 AF.Copy, accum_out=mu2S[:, i:i + 1]),
                    ('s_hev', cc, i), waits=w)
                bset(bk, ('s_hev', cc, i))
                sop(lambda s: s.activation(hgS[:, D:D + HR], hcS[:, i * HR:(i + 1) * HR],
                                           AF.Square, accum_out=var2S[:, i:i + 1]),
                    ('s_hsq', cc, i))

            def hg_vs(i):
                sl = i * HR
                w = [('s_rstd2', cc)] + ([hg_last[i]] if hg_last[i] else [])
                if cc == 0:
                    w.append(('s_acc', 1))
                vop(lambda v: v.tensor_scalar(hgS[:, sl:sl + HR],
                                              hcS[:, i * HR:(i + 1) * HR],
                                              mu2S[:, i:i + 1], rstd2S[:, i:i + 1],
                                              op0=ALU.subtract, op1=ALU.mult),
                    None, waits=w)
                vop(lambda v: v.tensor_mul(hgS[:, sl:sl + HR], hgS[:, sl:sl + HR],
                                           lngrep[:, :]))
                vop(lambda v: v.tensor_add(hgS[:, sl:sl + HR], hgS[:, sl:sl + HR],
                                           lnbrep[:, :]), ('v_aff', cc, i))
                sop(lambda s: s.activation(hgG[:, sl:sl + HR], hgS[:, sl:sl + HR], AF.Gelu),
                    ('s_hg', cc, i), waits=[('v_aff', cc, i)])

            def hg_dmaT():
                if cc >= 1:
                    env.wait('sync', ('pe_lg', cc - 1))
                for i in range(TT):
                    env.wait('sync', ('s_hg', cc, i))
                    for j in range(KH):
                        ins = (SY.dma_start_transpose(
                            hgT[:, j * CT + i * 128:j * CT + (i + 1) * 128],
                            hgG[:, i * HR + j * 128:i * HR + (j + 1) * 128])
                            if E('sync') else None)
                        key = ('dh_i', cc, i) if j == KH - 1 else None
                        env.mark('dh%d' % i, key, ins, 16)
                    hg_last[i] = ('pe_lg', cc)

            def pre_v_piece(i):
                w = [('dx', cc)] if i == 0 else []
                if i == 0 and cc >= 1:
                    w.append(('v_xn', cc - 1, TT - 1))
                sop(lambda s, ii=i: s.activation(hgS[:, 0:D], oview(cc, ii), AF.Copy,
                                                 accum_out=musS[:, ii:ii + 1]),
                    None, waits=w)
                sop(lambda s, ii=i: s.activation(hgS[:, 0:D], oview(cc, ii), AF.Square,
                                                 accum_out=varS[:, ii:ii + 1]),
                    ('s_acc', cc) if i == TT - 1 else None)
                if i == TT - 1:
                    vop(lambda v: v.tensor_scalar(musS[:, :], musS[:, :], 1.0 / D, None,
                                                  op0=ALU.mult), None, waits=[('s_acc', cc)])
                    vop(lambda v: v.tensor_mul(msqS[:, :], musS[:, :], musS[:, :]))
                    vop(lambda v: v.tensor_scalar(varS[:, :], varS[:, :], 1.0 / D, LN_EPS,
                                                  op0=ALU.mult, op1=ALU.add))
                    vop(lambda v: v.tensor_sub(varS[:, :], varS[:, :], msqS[:, :]),
                        ('v_var', cc))
                    sop(lambda s: s.activation(rstdS[:, :], varS[:, :], AF.Ln), None,
                        waits=[('v_var', cc)])
                    sop(lambda s: s.activation(rstdS[:, :], rstdS[:, :], AF.Exp, scale=-0.5),
                        ('s_rstd', cc))

            def s0():
                for i in range(TT):
                    w = [('s_rstd', cc)]
                    if i == 0 and cc >= 1:
                        w.extend(('dt_i', cc - 1, t) for t in range(TT))
                    if i == 0 and cc >= 3:
                        w.append(('pe_up_all', cc - 1, 0))
                    vop(lambda v, ii=i: v.tensor_scalar(xnS[:, ii * D:(ii + 1) * D],
                                                        oview(cc, ii),
                                                        musS[:, ii:ii + 1], rstdS[:, ii:ii + 1],
                                                        op0=ALU.subtract, op1=ALU.mult),
                        ('v_xn', cc, i), waits=w)
                xn_dmaT()

            def rstd2_batch():
                vop(lambda v: v.tensor_scalar(mu2S[:, :], mu2S[:, :], 1.0 / HR, None,
                                              op0=ALU.mult), None, waits=[('s_hsq', cc, TT - 1)])
                vop(lambda v: v.tensor_mul(msq2S[:, :], mu2S[:, :], mu2S[:, :]))
                vop(lambda v: v.tensor_scalar(var2S[:, :], var2S[:, :], 1.0 / HR, LN_EPS,
                                              op0=ALU.mult, op1=ALU.add))
                vop(lambda v: v.tensor_sub(var2S[:, :], var2S[:, :], msq2S[:, :]),
                    ('v_var2', cc))
                sop(lambda s: s.activation(rstd2S[:, :], var2S[:, :], AF.Ln), None,
                    waits=[('v_var2', cc)])
                sop(lambda s: s.activation(rstd2S[:, :], rstd2S[:, :], AF.Exp, scale=-0.5),
                    ('s_rstd2', cc))

            def s7():
                bwait(psG)
                for i in range(TT):
                    w = [('dh_i', cc, i)]
                    for kt in range(KH):
                        pop(lambda t, k=kt, ii=i: t.matmul(
                            psG[:, ii * EXP:(ii + 1) * EXP],
                            hgT[:, k * CT + ii * 128:k * CT + (ii + 1) * 128],
                            r2buf[:, k * EXP:(k + 1) * EXP], start=(k == 0), stop=False),
                            None, waits=(w if kt == 0 else []))
                    pop(lambda t, ii=i: t.matmul(psG[:, ii * EXP:(ii + 1) * EXP], on1bS[:, :],
                                                 r2brow[:, :], start=False, stop=True),
                        ('pe_lg', cc) if i == TT - 1 else None)
                for i in range(TT):
                    g = slice(i * EXP, (i + 1) * EXP)
                    vop(lambda v, gg=g: v.tensor_scalar(lgS[:, gg], psG[:, gg], invt[:, 0:1],
                                                        None, op0=ALU.mult),
                        ('v_lge', cc) if i == TT - 1 else None,
                        waits=[('pe_lg', cc)] if i == 0 else [])
                    vop(lambda v, gg=g: v.tensor_add(lgS[:, gg], lgS[:, gg], rbrep[:, :]))
                    vop(lambda v, gg=g, ii=i: v.reduce_max(nmxS[:, ii:ii + 1], lgS[:, gg],
                                                           axis=AX.X, negate=True),
                        ('v_nmx', cc, i))
                bset(psG, ('v_lge', cc))

            def s8a():
                for i in range(TT):
                    g = slice(i * EXP, (i + 1) * EXP)
                    sop(lambda s, gg=g, ii=i: s.activation(expS[:, gg], lgS[:, gg], AF.Exp,
                                                           bias=nmxS[:, ii:ii + 1]),
                        ('s_exp', cc, i), waits=[('v_nmx', cc, TT - 1)] if i == 0 else [])

            def s8a_piece(i):
                if True:
                    g = slice(i * EXP, (i + 1) * EXP)
                    pg = slice(po + i * EXP, po + (i + 1) * EXP)
                    vop(lambda v, gg=g, ii=i: v.reduce_sum(seS[:, ii:ii + 1], expS[:, gg],
                                                           axis=AX.X),
                        None, waits=[('s_exp', cc, i)])
                    vop(lambda v, ii=i: v.reciprocal(rseS[:, ii:ii + 1], seS[:, ii:ii + 1]))
                    vop(lambda v, gg=g, pgg=pg, ii=i: v.tensor_scalar(prS[:, pgg], expS[:, gg],
                                                                      rseS[:, ii:ii + 1],
                                                                      None, op0=ALU.mult))
                    vop(lambda v, pgg=pg, ii=i: v.reduce_max(t1S[:, ii:ii + 1], prS[:, pgg],
                                                             axis=AX.X))
                    vop(lambda v, pgg=pg, ii=i: v.tensor_scalar(eqbS[:, :], prS[:, pgg],
                                                                t1S[:, ii:ii + 1], BIG,
                                                                op0=ALU.is_equal, op1=ALU.mult))
                    vop(lambda v, pgg=pg: v.tensor_sub(pmS[:, :], prS[:, pgg], eqbS[:, :]))
                    vop(lambda v, ii=i: v.reduce_max(t2S[:, ii:ii + 1], pmS[:, :], axis=AX.X))
                    vop(lambda v, pgg=pg, ii=i: v.scalar_tensor_tensor(prS[:, pgg], prS[:, pgg],
                                                                       t2S[:, ii:ii + 1],
                                                                       prS[:, pgg],
                                                                       op0=ALU.is_ge,
                                                                       op1=ALU.mult),
                        ('v_pr', cc, i))

            def s8b():
                bwait(psG)
                for i in range(TT):
                    pop(lambda t, ii=i: t.transpose(psG[0:EXP, ii * 128:(ii + 1) * 128],
                                                    prS[:, po + ii * EXP:po + (ii + 1) * EXP],
                                                    idS[:, :]),
                        ('pe_pT', cc) if i == TT - 1 else None, waits=[('v_pr', cc, i)])
                vop(lambda v: v.tensor_copy(pTbf[:, to:to + CT], psG[0:EXP, 0:CT]),
                    ('v_pTe', cc), waits=[('pe_pT', cc)])
                bset(psG, ('v_pTe', cc))

            pre = [None] * 10
            mid = [None] * 10
            vp = [None] * 10
            vp[0] = [lambda ii=i: pre_v_piece(ii) for i in range(TT)]
            pre[1] = s0
            pre[2] = lambda: hmm_pe(0)
            mid[2] = lambda: hmm_v(0)
            pre[3] = lambda: hmm_pe(1)
            mid[3] = lambda: hmm_v(1)
            pre[4] = lambda: hmm_pe(2)
            mid[4] = lambda: hmm_v(2)
            pre[5] = lambda: hmm_pe(3)
            mid[5] = lambda: hmm_v(3)
            vp[5] = [rstd2_batch]
            vp[6] = [lambda ii=i: hg_vs(ii) for i in range(TT)]
            pre[7] = hg_dmaT
            pre[8] = s7
            mid[8] = lambda: (s8a(), s8a_piece(0), s8a_piece(1))
            vp[8] = [lambda: s8a_piece(2), lambda: s8a_piece(3)]
            pre[9] = s8b
            return pre, mid, vp

        def consume(c, produce_next):
            par = c % 2
            xo = par * KD * CT
            po = par * TT * EXP
            to = par * CT

            if c >= 1:
                pp = (c - 1) % 3
                env.wait('sync', ('v_dnall', c - 1))
                ins = (SY.dma_start(
                    dY[(c - 1) * CT:c * CT, :].rearrange("(tt p) d -> p tt d", p=128),
                    oac[:, pp * TT * D:(pp + 1) * TT * D].rearrange("p (tt d) -> p tt d", tt=TT))
                    if E('sync') else None)
                env.mark('do', ('do', c - 1), ins, 16)
            if c + 2 < NCH:
                cn = c + 2
                pn = cn % 3
                env.wait('sync', ('dx', cn - 1))
                if cn >= 3:
                    env.wait('sync', ('do', cn - 3))
                ins = (SY.dma_start(
                    oac[:, pn * TT * D:(pn + 1) * TT * D].rearrange("p (tt d) -> p tt d", tt=TT),
                    dx_[cn * CT:(cn + 1) * CT, :].rearrange("(tt p) d -> p tt d", p=128))
                    if E('sync') else None)
                env.mark('dx', ('dx', cn), ins, 16)

            PXc = PX0 if c == 0 else PX
            for pi, (kind, idx) in enumerate(PXc):
                prev = (c, pi - 1) if pi > 0 else ((c - 1, LAST_P) if c > 0 else None)
                nf = FE if kind == 'e' else FQ
                fw = 1536 if kind == 'e' else 1024
                if c == 0:
                    if prev is not None:
                        env.wait('gp', ('pe_up_all',) + prev)
                    env.wait('gp', ('st1', pi))
                    dst1 = w1buf[:, 0:KD * fw].rearrange("p (kt f) -> p kt f", kt=KD)
                    ins = G.dma_start(dst1, w1_src(kind, idx)) if E('gp') else None
                    env.mark('w1g', ('gw1', c, pi), ins, 16)
                if c <= 1:
                    gseq = c * NPX + pi
                    if gseq >= 2:
                        p2 = gseq - 2
                        env.wait('gp', ('pe_dn_all', p2 // NPX, p2 % NPX))
                    if c == 1 and pi >= 2:
                        env.wait('gp', ('st2s', pi - 2))
                    w2o = (gseq % 2) * FE * D
                    dst2 = w2buf[:, w2o:w2o + nf * D].rearrange("p (fk d) -> p fk d", fk=nf)
                    ins = G.dma_start(dst2, w2_src(kind, idx)) if E('gp') else None
                    env.mark('w2g' if gseq % 2 == 0 else 'w3g', ('gw2', c, pi), ins, 16)
                if c == 0:
                    env.wait('gp', ('gw1', c, pi))
                    wid = PX.index((kind, idx))
                    ins = G.dma_start(scr1[wid, :, 0:KD * fw], w1buf[:, 0:KD * fw]) if E('gp') else None
                    env.mark('s1', ('st1', pi + 1), ins, 16)

            HKT = 2
            HC = HKT * 1536

            def emit_gw1h(cw):
                for t in range(TT):
                    env.wait('s', ('dt_i', cw, t))
                ins = S.dma_start(
                    xnS[:, 0:HC].rearrange("p (kt f) -> p kt f", kt=HKT),
                    scr1[0, :, 0:HC]) if E('s') else None
                env.mark('w1h', ('gw1h', cw), ins, 16)

            def emit_gw1(cw, pw):
                prev = (cw, pw - 1) if pw > 0 else (cw - 1, LAST_P)
                env.wait('s', ('pe_up_all',) + prev)
                if cw == 1:
                    env.wait('s', ('st1', NPX))
                fwv = 1536 if PX[pw][0] == 'e' else 1024
                lo = HC if (pw == 0 and cw >= 2) else 0
                dst1 = w1buf[:, lo:KD * fwv].rearrange("p (kt f) -> p kt f", kt=KD - (HKT if lo else 0))
                ins = S.dma_start(dst1, scr1[pw, :, lo:KD * fwv]) if E('s') else None
                env.mark('w1', ('gw1', cw, pw), ins, 16)

            def emit_scr2_store(pw):
                nfw = FE if PX[pw][0] == 'e' else FQ
                w2o = ((NPX + pw) % 2) * FE * D
                env.wait('sync', ('gw2', 1, pw))
                if pw >= 1:
                    env.wait('sync', ('st2s', pw - 1))
                ins = SY.dma_start(scr2[pw, :, 0:nfw * D], w2buf[:, w2o:w2o + nfw * D]) if E('sync') else None
                env.mark('s2', ('st2s', pw), ins, 16)

            def emit_gw2(cw, pw):
                gseq = cw * NPX + pw
                p2 = gseq - 2
                env.wait('s', ('pe_dn_all', p2 // NPX, p2 % NPX))
                if cw == 2:
                    env.wait('s', ('st2s', LAST_P))
                nfw = FE if PX[pw][0] == 'e' else FQ
                w2o = (gseq % 2) * FE * D
                dst2 = w2buf[:, w2o:w2o + nfw * D].rearrange("p (fk d) -> p fk d", fk=nfw)
                ins = S.dma_start(dst2, scr2[pw, :, 0:nfw * D]) if E('s') else None
                env.mark('w2' if gseq % 2 == 0 else 'w3', ('gw2', cw, pw), ins, 16)

            for pi, (kind, idx) in enumerate(PXc):
                if produce_next is not None and produce_next[0][pi] is not None:
                    produce_next[0][pi]()
                nf = FE if kind == 'e' else FQ
                fw = 1536 if kind == 'e' else 1024
                for f in range(nf):
                    bk = UPB[upcnt[0] % 3]
                    upcnt[0] += 1
                    w = []
                    if c == 0 and pi == 0 and f < 3:
                        w.append(('v_invt',))
                    if f == 0:
                        w.append(('gw1', c, pi))
                        if pi == 0:
                            w.extend(('dt_i', c, t) for t in range(TT))
                            if c >= 2:
                                w.append(('gw1h', c))
                    if upbank[id(bk)] is not None:
                        w.append(upbank[id(bk)])
                    for kt in range(KD):
                        key = ('pe_up', c, pi, f) if kt == KD - 1 else None
                        wsrc = xnS if (c >= 2 and pi == 0 and kt < 2) else w1buf
                        pop(lambda t, k=kt, b=bk, ff=f, ws=wsrc: t.matmul(
                            b[:, :], ws[:, k * fw + ff * 128:k * fw + (ff + 1) * 128],
                            xnT[:, xo + k * CT:xo + (k + 1) * CT], start=(k == 0),
                            stop=(k == KD - 1)),
                            key, waits=(w if kt == 0 else []))
                    if kind == 'e':
                        af = AF.Gelu if idx % 2 == 0 else AF.Silu
                        sop(lambda s, b=bk, ff=f: s.activation(
                            H_sb[:, ff * CT:(ff + 1) * CT], b[:, :], af,
                            bias=b1T[:, idx * FE + ff:idx * FE + ff + 1]),
                            ('s_ue', c, pi, f),
                            waits=[('pe_up', c, pi, f)] + ([('v_b1T',)] if c == 0 and pi == 0 and f == 0 else []))
                        upbank[id(bk)] = ('s_ue', c, pi, f)
                    else:
                        sop(lambda s, b=bk, ff=f: s.activation(
                            H_sb[:, ff * CT:(ff + 1) * CT], b[:, :], AF.Relu,
                            bias=fb1T[:, idx * FQ + ff:idx * FQ + ff + 1]),
                            ('s_ue', c, pi, f),
                            waits=[('pe_up', c, pi, f)] + ([('v_fb1T',)] if c == 0 and f == 0 else []))
                        upbank[id(bk)] = ('s_ue', c, pi, f)
                env.alias(('pe_up_all', c, pi), ('pe_up', c, pi, nf - 1))
                if produce_next is not None and produce_next[1][pi] is not None:
                    produce_next[1][pi]()
                if pi + 1 < NPX:
                    if c >= 1:
                        emit_gw1(c, pi + 1)
                    if c >= 1 and pi == 2 and c + 1 < NCH:
                        emit_gw1h(c + 1)
                    if c == 1:
                        emit_scr2_store(pi)
                    elif c >= 2:
                        emit_gw2(c, pi + 1)
                elif c + 1 < NCH:
                    emit_gw1(c + 1, 0)
                    if c == 1:
                        emit_scr2_store(LAST_P)
                    if c >= 1:
                        emit_gw2(c + 1, 0)
                def act_mark(fk):
                    return ('s_ue', c, pi, fk)
                w2o = ((c * NPX + pi) % 2) * FE * D
                for i in range(TT):
                    pr = DNP[i % 2]
                    bC = pr[:, 0:512]
                    bD = pr[:, 512:1024]
                    bwait(pr)
                    first = True
                    if kind == 'f' and idx == 0:
                        pop(lambda t, ii=i: t.matmul(bC,
                                                     pTbf[:, to + ii * 128:to + (ii + 1) * 128],
                                                     b2sb[:, 0:512], start=True, stop=False),
                            None, waits=([('v_pTe', c)] if i == 0 else []))
                        pop(lambda t, ii=i: t.matmul(bD,
                                                     pTbf[:, to + ii * 128:to + (ii + 1) * 128],
                                                     b2sb[:, 512:1024], start=True, stop=False))
                        pop(lambda t: t.matmul(bC, on1bS[:, :], fb2row[:, 0:512],
                                               start=False, stop=False))
                        pop(lambda t: t.matmul(bD, on1bS[:, :], fb2row[:, 512:1024],
                                               start=False, stop=False))
                        first = False
                    for fk in range(nf):
                        stv = first and fk == 0
                        spv = fk == nf - 1
                        w = []
                        if i == 0:
                            w.append(act_mark(fk))
                            if fk == 0:
                                w.append(('gw2', c, pi))
                        pop(lambda t, k=fk, s0v=stv, s1v=spv, ii=i: t.matmul(
                            bC, H_sb[:, k * CT + ii * 128:k * CT + (ii + 1) * 128],
                            w2buf[:, w2o + k * D:w2o + k * D + 512], start=s0v, stop=s1v),
                            None, waits=w)
                        key = ('pe_dn', c, pi, i) if spv else None
                        pop(lambda t, k=fk, s0v=stv, s1v=spv, ii=i: t.matmul(
                            bD, H_sb[:, k * CT + ii * 128:k * CT + (ii + 1) * 128],
                            w2buf[:, w2o + k * D + 512:w2o + (k + 1) * D], start=s0v, stop=s1v), key)
                    if kind == 'e':
                        pc = prS[:, po + i * EXP + idx:po + i * EXP + idx + 1]
                        vop(lambda v, ii=i, p_=pc, b=pr: v.scalar_tensor_tensor(
                            oview(c, ii), b[:, :], p_, oview(c, ii),
                            op0=ALU.mult, op1=ALU.add),
                            ('v_dn', c, pi, i), waits=[('pe_dn', c, pi, i)])
                    else:
                        vop(lambda v, ii=i, b=pr: v.tensor_add(oview(c, ii),
                                                               oview(c, ii), b[:, :]),
                            ('v_dn', c, pi, i), waits=[('pe_dn', c, pi, i)])
                    bset(pr, ('v_dn', c, pi, i))
                    if produce_next is not None:
                        vps = produce_next[2][pi]
                        if vps is not None and i < len(vps):
                            vps[i]()
                env.alias(('pe_dn_all', c, pi), ('pe_dn', c, pi, TT - 1))
            env.alias(('v_dnall', c), ('v_dn', c, LAST_P, TT - 1))

        ins = (SY.dma_start(
            oac[:, 0:TT * D].rearrange("p (tt d) -> p tt d", tt=TT),
            dx_[0:CT, :].rearrange("(tt p) d -> p tt d", p=128))
            if E('sync') else None)
        env.mark('dx', ('dx', 0), ins, 16)
        env.wait('sync', ('dx', 0))
        ins = (SY.dma_start(
            oac[:, TT * D:2 * TT * D].rearrange("p (tt d) -> p tt d", tt=TT),
            dx_[CT:2 * CT, :].rearrange("(tt p) d -> p tt d", p=128))
            if E('sync') else None)
        env.mark('dx', ('dx', 1), ins, 16)

        pre0, mid0, vp0 = produce_stages(0)
        for p in vp0[0]:
            p()
        for j in range(6):
            if pre0[j] is not None:
                pre0[j]()
            if mid0[j] is not None:
                mid0[j]()
            if j >= 1 and vp0[j] is not None:
                for p in vp0[j]:
                    p()
            if j == 5:
                setup_compute()

        def chain(*fns):
            fns = [f for f in fns if f is not None]
            if not fns:
                return None
            def run():
                for f in fns:
                    f()
            return run

        def runlist(lst):
            if lst is None:
                return None
            def run():
                for p in lst:
                    p()
            return run

        pre1, mid1, vp1 = produce_stages(1)
        d_pre1 = list(pre1)
        d_mid1 = list(mid1)
        d_pre1[1] = chain(runlist(vp0[6]), pre0[7], pre1[1])
        d_pre1[2] = chain(pre0[8], pre1[2])
        d_mid1[2] = chain(mid0[8], runlist(vp0[8]), mid1[2])
        d_pre1[3] = chain(pre0[9], pre1[3])
        consume(0, (d_pre1, d_mid1, vp1))
        for c in range(1, NCH):
            produce_next = produce_stages(c + 1) if c + 1 < NCH else None
            consume(c, produce_next)

        c = NCH - 1
        pp = c % 3
        env.wait('sync', ('v_dnall', c))
        ins = (SY.dma_start(
            dY[c * CT:(c + 1) * CT, :].rearrange("(tt p) d -> p tt d", p=128),
            oac[:, pp * TT * D:(pp + 1) * TT * D].rearrange("p (tt d) -> p tt d", tt=TT))
            if E('sync') else None)
        env.mark('do', ('do', c), ins, 16)
        env.wait('sync', ('do', c))

    return nc, st, sems, emit


def build():
    nc, st, SEMS, emit = build_nc()
    marks = {}
    emit(Env(None, marks))
    with st:
        with nc.Block() as blk:
            def run(eng_name):
                def body(engine):
                    env = Env(eng_name, marks, SEMS, {eng_name: engine})
                    emit(env)
                return body
            blk.tensor(run('pe'))
            blk.vector(run('v'))
            blk.scalar(run('s'))
            blk.sync(run('sync'))
            blk.gpsimd(run('gp'))
    return nc


_NC_CACHE = None


def kernel(**inputs):
    global _NC_CACHE
    gi = lambda k: np.ascontiguousarray(np.asarray(inputs[k], dtype=np.float32))
    x = gi("L0_attn_x").reshape(NCORE * TOK, D)
    shards = np.split(x, NCORE, axis=0)

    common = dict(
        aW1=gi("L0_attn_W1"), aB1=gi("L0_attn_b1").reshape(EXP * FE, 128),
        aW2=gi("L0_attn_W2"), aB2=gi("L0_attn_b2"),
        r1w=gi("L0_attn_r1_w"), r1b=gi("L0_attn_r1_b").reshape(1, HR),
        lng=gi("L0_attn_ln_g").reshape(1, HR), lnb=gi("L0_attn_ln_b").reshape(1, HR),
        r2w=gi("L0_attn_r2_w"), r2b=gi("L0_attn_r2_b").reshape(1, EXP),
        temp=gi("L0_attn_temp").reshape(1, 1), rbias=gi("L0_attn_route_bias").reshape(1, EXP),
        fW1=gi("L0_ffn_W1"), fB1=gi("L0_ffn_b1").reshape(32, 128),
        fW2=gi("L0_ffn_W2"), fB2=gi("L0_ffn_b2").reshape(1, D),
        ident=np.eye(128, dtype=np.float32),
        ones1=np.ones((1, 128), dtype=np.float32),
        ones1b=np.ones((1, 128), dtype=ml_dtypes.bfloat16),
    )
    if _NC_CACHE is None:
        _NC_CACHE = build()
    nc = _NC_CACHE
    in_maps = [dict(common, x=shards[ci]) for ci in range(NCORE)]
    res = run_bass_kernel_spmd(nc, in_maps, core_ids=list(range(NCORE)))
    y = np.concatenate([r["y"] for r in res.results], axis=0)
    return y.reshape(NCORE, TOK, D).astype(np.float32, copy=False)
